# version 09
# baseline (speedup 1.0000x reference)
import numpy as np
import ml_dtypes
from contextlib import ExitStack

import concourse.bass as bass
import concourse.mybir as mybir
from concourse.bass_utils import run_bass_kernel_spmd

F32 = mybir.dt.float32
BF16 = mybir.dt.bfloat16
AF = mybir.ActivationFunctionType
ALU = mybir.AluOpType
AX = mybir.AxisListType


class CFG:
    def __init__(self, B=2, S=2048, D=1024, F=4096, E=8, TR=1024, n_cores=8):
        self.B, self.S, self.D, self.F, self.E, self.TR = B, S, D, F, E, TR
        self.n_cores = n_cores
        self.DK = 64
        self.H = D // 64
        self.DC = D // 128
        self.FC = F // 128
        self.TW = 512
        self.NW = S // self.TW
        self.T = B * S
        self.NT = self.T // 128
        self.NP = self.T // TR
        self.TRT = TR // 128
        self.T5 = TR // 512
        self.NTB = S // 128
        assert D % 128 == 0 and F % 512 == 0 and S % 512 == 0 and TR % 512 == 0
        assert self.T % TR == 0 and self.S % TR == 0


NCH = 8
ENG = (("pe", "ve", "se", "gp")
       + tuple(f"syc{k}" for k in range(NCH))
       + tuple(f"gqc{k}" for k in range(NCH)))
PHYS = {"pe": "tensor", "ve": "vector", "se": "scalar", "gp": "gpsimd"}
for _k in range(NCH):
    PHYS[f"syc{_k}"] = "sync"
    PHYS[f"gqc{_k}"] = "gpsimd"


def I(method, *args, **kw):
    def f(e):
        return [getattr(e, method)(*args, **kw)]
    return f


def SEQ(instrs, inc_idx=None):
    if inc_idx is None:
        inc_idx = [len(instrs) - 1]
    inc_set = set(i % len(instrs) for i in inc_idx)

    def f(e):
        out = []
        for n, (m, a, kw) in enumerate(instrs):
            ins = getattr(e, m)(*a, **kw)
            if n in inc_set:
                out.append(ins)
        return out
    return f


class Plan:
    def __init__(self, nc, es):
        self.nc = nc
        self.units = []
        self.idx = {e: 0 for e in ENG}
        self.w = {}
        self.r = {}
        self.sems = {e: es.enter_context(nc.semaphore(f"sem_{e}")) for e in ENG}
        self.floors = {e: {} for e in ENG}
        self.dmacnt = {"sy": 0, "gq": 0}
        self.pe_defer = None
        self.defer_map = {}

    def barrier(self):
        snap = dict(self.idx)
        for e in ENG:
            f = self.floors[e]
            for e2 in ENG:
                if e2 != e and snap[e2] > 0:
                    f[e2] = max(f.get(e2, 0), snap[e2])

    def unit(self, eng, fn, reads=(), writes=(), dma=False, nincs=1,
             defer_drain=False):
        if dma:
            ch = self.dmacnt[eng] % NCH
            self.dmacnt[eng] += 1
            eng = f"{eng}c{ch}"
            writes = list(writes) + [f"_{eng}"]
        need = {}

        def _need(e2, i2):
            if e2 == "pe" and eng == "pe":
                return
            if i2 > need.get(e2, 0):
                need[e2] = i2

        for k in reads:
            p = self.w.get(k)
            if p:
                _need(*p)
        for k in writes:
            p = self.w.get(k)
            if p:
                _need(*p)
            for e2, i2 in self.r.get(k, {}).items():
                _need(e2, i2)
        for e2, i2 in self.floors[eng].items():
            if i2 > need.get(e2, 0):
                need[e2] = i2
        self.floors[eng] = {}
        has_drain = eng in ("pe", "ve", "se", "gp") and not defer_drain
        if has_drain:
            nincs += 1
        inc = 16 if dma else 1
        if eng == "pe" and self.pe_defer is not None:
            self.defer_map[self.pe_defer] = self.idx[eng] + 1
            self.pe_defer = None
        self.idx[eng] += inc * nincs
        my = (eng, self.idx[eng])
        if eng == "pe" and defer_drain:
            self.pe_defer = my[1]
        for k in writes:
            self.w[k] = my
            self.r[k] = {}
        for k in reads:
            rr = self.r.setdefault(k, {})
            if rr.get(eng, 0) < my[1]:
                rr[eng] = my[1]
        self.units.append((eng, sorted(need.items()), fn, inc, nincs,
                           has_drain if eng in ("pe", "ve", "se", "gp") else False))

    def emit(self, blk):
        assert self.pe_defer is None, "last PE unit cannot defer its drain"
        streams = {v: [] for v in set(PHYS.values())}
        for u in self.units:
            streams[PHYS[u[0]]].append(u)

        def mk(items):
            def body(e):
                hw = {}
                for eng, waits, fn, inc, nincs, has_drain in items:
                    for e2, v in waits:
                        if e2 == "pe":
                            v = self.defer_map.get(v, v)
                        if hw.get(e2, 0) >= v:
                            continue
                        e.wait_ge(self.sems[e2], v)
                        hw[e2] = v
                    instrs = fn(e)
                    if has_drain:
                        instrs = instrs + [e.drain()]
                    assert len(instrs) == nincs, (eng, len(instrs), nincs)
                    for ins in instrs:
                        ins.then_inc(self.sems[eng], inc)
            return body

        for sname in ("sync", "gpsimd", "tensor", "scalar", "vector"):
            if streams.get(sname):
                getattr(blk, sname)(mk(streams[sname]))


def build_program(cfg: CFG, dbg=False) -> bass.Bass:
    c = cfg
    nc = bass.Bass("TRN2", target_bir_lowering=False)

    x_in = nc.dram_tensor("L0_attn_x", [c.B, c.S, c.D], F32, kind="ExternalInput").ap()
    wq_d = nc.dram_tensor("L0_attn_Wq", [c.D, c.D], F32, kind="ExternalInput").ap()
    bq_d = nc.dram_tensor("L0_attn_bq", [c.D], F32, kind="ExternalInput").ap()
    wk_d = nc.dram_tensor("L0_attn_Wk", [c.D, c.D], F32, kind="ExternalInput").ap()
    bk_d = nc.dram_tensor("L0_attn_bk", [c.D], F32, kind="ExternalInput").ap()
    wv_d = nc.dram_tensor("L0_attn_Wv", [c.D, c.D], F32, kind="ExternalInput").ap()
    bv_d = nc.dram_tensor("L0_attn_bv", [c.D], F32, kind="ExternalInput").ap()
    wo_d = nc.dram_tensor("L0_attn_Wo", [c.D, c.D], F32, kind="ExternalInput").ap()
    bo_d = nc.dram_tensor("L0_attn_bo", [c.D], F32, kind="ExternalInput").ap()
    wg8_d = nc.dram_tensor("L0_ffn_w_gate", [c.D, c.E], F32, kind="ExternalInput").ap()
    wgu_d = nc.dram_tensor("L0_ffn_w_gate_up", [c.E, c.D, 2 * c.F], F32,
                           kind="ExternalInput").ap()
    wdn_d = nc.dram_tensor("L0_ffn_w_down", [c.E, c.F, c.D], F32,
                           kind="ExternalInput").ap()
    out_d = nc.dram_tensor("out", [c.B, c.S, c.D], F32, kind="ExternalOutput").ap()
    scr_kind = "ExternalOutput" if dbg else "Internal"
    dbg_full = dbg is True
    dbg_lv = {False: 0, 'late': 1, 'v1': 2, 'v2': 3, 'v3': 4, 'v4': 5, 'v5': 6, 'v6': 7, True: 9}[dbg]
    x1_d = nc.dram_tensor("x1_scr", [c.T, c.D], F32, kind=scr_kind).ap()
    xn1_d = nc.dram_tensor("xn1_scr", [c.DC, 128, c.T], BF16, kind=scr_kind).ap()
    dbg_t = {}
    if dbg:
        for nm, shape, dt_ in (("dbg_xn", [128, c.DC * c.S], BF16),
                               ("dbg_q", [128, c.DC * c.S], BF16),
                               ("dbg_k", [128, c.DC * c.S], BF16),
                               ("dbg_v", [128, c.NTB * c.H * 65], BF16),
                               ("dbg_ao", [128, c.DC * c.S], BF16),
                               ("dbg_mhat", [1, c.S], BF16),
                               ("dbg_kb2", [1, c.H], F32),
                               ("dbg_gates", [128, c.NT * c.E], F32),
                               ("dbg_act", [128, c.FC * c.TR], BF16),
                               ("dbg_stat", [128, 16], F32),
                               ("dbg_xntm", [128, 2 * 1024], BF16)):
            dbg_t[nm] = nc.dram_tensor(nm, shape, dt_, kind="ExternalOutput").ap()

    ident_dram = nc.inline_tensor(np.eye(128, dtype=ml_dtypes.bfloat16), name="ident_c")

    def x_tile(t):
        b, r = t // c.NTB, (t % c.NTB) * 128
        return x_in[b, r:r + 128, :]

    es = ExitStack()
    with es:
        P = Plan(nc, es)
        U = P.unit

        def sb(name, shape, dtype, stk=es):
            return stk.enter_context(nc.sbuf_tensor(name, shape, dtype))

        ident = sb("identsb", [128, 128], BF16)
        onesb = sb("onesb", [1, 128], BF16)
        negob = sb("negob", [1, 128], BF16)
        onescol = sb("onescol", [128, 1], BF16)
        onesm = sb("onesm", [128, 128], BF16)
        onesfr = sb("onesfr", [1, 128], F32)
        onesf = sb("onesf", [1, 1], F32)
        epsap = sb("epsap", [128, 1], F32)
        maskb = sb("maskb", [128, 896], BF16)
        sqb = sb("sqb", [128, c.D], F32)
        bqc = sb("bqc", [128, c.DC], F32)
        bkc = sb("bkc", [128, c.DC], F32)
        wg8 = sb("wg8", [128, c.DC * c.E], BF16)
        gates = sb("gates", [128, c.NT * c.E], F32)
        stat = sb("stat", [128, 16], F32)
        gstat = sb("gstat", [128, 8], F32)
        lgs = sb("lgs", [128, 2 * c.E], F32)

        attn_es = ExitStack()
        bv_bc = sb("bv_bc", [128, c.D], F32, attn_es)
        bo_bc = sb("bo_bc", [128, c.D], F32, attn_es)
        ph0_es = ExitStack()
        brow = sb("brow", [1, 4 * c.D], F32, ph0_es)
        mwk = sb("mwk", [128, 896], F32, ph0_es)
        U("sy", I("dma_start", ident[:, :], ident_dram.ap()), writes=["ident"], dma=True)
        U("sy", I("dma_start", brow[0:1, 0:c.D], bq_d[None, :]), writes=["brq"], dma=True)
        U("sy", I("dma_start", brow[0:1, c.D:2 * c.D], bk_d[None, :]), writes=["brk"], dma=True)
        U("sy", I("dma_start", brow[0:1, 2 * c.D:3 * c.D], bv_d[None, :]), writes=["brv"], dma=True)
        U("sy", I("dma_start", brow[0:1, 3 * c.D:4 * c.D], bo_d[None, :]), writes=["bro"], dma=True)
        U("gq", I("dma_start", wg8[:, :].rearrange("p (cc e) -> p cc e", cc=c.DC),
                  wg8_d.rearrange("(cc p) e -> p cc e", p=128)), writes=["wg8"], dma=True)
        U("gp", I("iota", mwk[:, :], [[1, 896]], channel_multiplier=-1,
                  allow_small_or_imprecise_dtypes=True), writes=["mwk"])
        U("ve", I("tensor_scalar", mwk[:, :], mwk[:, :], 384.0, None, op0=ALU.is_ge),
          reads=["mwk"], writes=["mwk"])
        U("ve", I("tensor_copy", maskb[:, :], mwk[:, :]), reads=["mwk"], writes=["maskb"])
        U("ve", I("memset", onesb[:, :], 1.0), writes=["onesb"])
        U("ve", I("memset", negob[:, :], -1.0), writes=["negob"])
        U("ve", I("memset", onescol[:, :], 1.0), writes=["onescol"])
        U("ve", I("memset", onesm[:, :], 1.0), writes=["onesm"])
        U("ve", I("memset", onesfr[:, :], 1.0), writes=["onesfr"])
        U("ve", I("memset", onesf[:, :], 1.0), writes=["onesf"])
        U("ve", I("memset", epsap[:, :], 1e-5), writes=["epsap"])

        with nc.psum_tensor("pbias", [128, 2 * c.DC], F32) as pbias, \
             nc.psum_tensor("pbcA", [128, 512], F32) as pbcA, \
             nc.psum_tensor("pbcB", [128, 512], F32) as pbcB:
            g = []
            for j in range(c.DC):
                g.append(("matmul", (pbias[:, j:j + 1],
                                     brow[0:1, j * 128:(j + 1) * 128],
                                     onesf[0:1, 0:1]), dict(start=True, stop=True)))
            for j in range(c.DC):
                g.append(("matmul", (pbias[:, c.DC + j:c.DC + j + 1],
                                     brow[0:1, c.D + j * 128:c.D + (j + 1) * 128],
                                     onesf[0:1, 0:1]), dict(start=True, stop=True)))
            U("pe", SEQ(g, inc_idx=list(range(len(g)))),
              reads=["brq", "brk", "onesf"], writes=["pbias"],
              nincs=len(g))
            U("ve", I("tensor_copy", bqc[:, :], pbias[:, 0:c.DC]),
              reads=["pbias"], writes=["bqc"])
            U("ve", I("tensor_copy", bkc[:, :], pbias[:, c.DC:2 * c.DC]),
              reads=["pbias"], writes=["bkc"])
            for (src_off, dstbuf, rk) in ((2 * c.D, bv_bc, "brv"), (3 * c.D, bo_bc, "bro")):
                for half in range(c.D // 512):
                    pb = pbcA if half % 2 == 0 else pbcB
                    pk = f"pbc{half % 2}"
                    U("pe", I("matmul", pb[:, :], onesfr[0:1, :],
                              brow[0:1, src_off + half * 512:src_off + (half + 1) * 512],
                              start=True, stop=True),
                      reads=["onesfr", rk], writes=[pk])
                    U("ve", I("tensor_copy", dstbuf[:, half * 512:(half + 1) * 512],
                              pb[:, :]), reads=[pk], writes=[rk + "bc"])
            P.barrier()
        ph0_es.close()

        for b in range(c.B):
            with ExitStack() as ab:
                q_fm = sb(f"q_fm{b}", [128, c.DC * c.S], BF16, ab)
                k_fm = sb(f"k_fm{b}", [128, c.DC * c.S], BF16, ab)
                v_tm = sb(f"v_tm{b}", [128, c.NTB * c.H * 65], BF16, ab)
                xstr = sb(f"xstr{b}", [128, 4 * c.D], F32, ab)

                def ln_chain(i, xk, xs, xnsl, xnkey, x1key=None):
                    sk = f"st{i % 2}"
                    so = (i % 2) * 8
                    src = [x1key] if x1key else [xk]
                    U("ve", I("reduce_sum", stat[:, so:so + 1], xs, axis=AX.X),
                      reads=src, writes=[sk + "s0"])
                    U("ve", I("tensor_tensor", sqb[:, :], xs, xs, op=ALU.mult),
                      reads=src, writes=["sqb"])
                    U("ve", I("reduce_sum", stat[:, so + 1:so + 2], sqb[:, :], axis=AX.X),
                      reads=["sqb"], writes=[sk + "s1"])
                    U("ve", I("tensor_scalar", stat[:, so + 2:so + 3],
                              stat[:, so:so + 1], 1.0 / c.D, None, op0=ALU.mult),
                      reads=[sk + "s0"], writes=[sk + "m"])
                    U("ve", I("tensor_tensor", stat[:, so + 3:so + 4],
                              stat[:, so + 2:so + 3], stat[:, so + 2:so + 3],
                              op=ALU.mult),
                      reads=[sk + "m"], writes=[sk + "m2"])
                    U("ve", I("scalar_tensor_tensor", stat[:, so + 4:so + 5],
                              stat[:, so + 1:so + 2], 1.0 / c.D,
                              stat[:, so + 3:so + 4], op0=ALU.mult, op1=ALU.subtract),
                      reads=[sk + "s1", sk + "m2"], writes=[sk + "v"])
                    U("se", I("activation", stat[:, so + 5:so + 6],
                              stat[:, so + 4:so + 5], AF.Ln, bias=epsap[:, 0:1],
                              scale=1.0),
                      reads=[sk + "v", "epsap"], writes=[sk + "ln"])
                    U("se", I("activation", stat[:, so + 6:so + 7],
                              stat[:, so + 5:so + 6], AF.Exp, bias=0.0, scale=-0.5),
                      reads=[sk + "ln"], writes=[sk + "r"])
                    U("ve", I("scalar_tensor_tensor", stat[:, so + 7:so + 8],
                              stat[:, so + 2:so + 3], -1.0, stat[:, so + 6:so + 7],
                              op0=ALU.mult, op1=ALU.mult),
                      reads=[sk + "m", sk + "r"], writes=[sk + "mb"])
                    U("ve", I("tensor_scalar", xnsl, xs,
                              stat[:, so + 6:so + 7], stat[:, so + 7:so + 8],
                              op0=ALU.mult, op1=ALU.add),
                      reads=src + [sk + "mb", sk + "r"], writes=[xnkey])

                with ExitStack() as pb_, \
                     nc.psum_tensor(f"ptpA{b}", [128, 128], BF16) as ptpA, \
                     nc.psum_tensor(f"ptpB{b}", [128, 128], BF16) as ptpB:
                    xn_fm = sb(f"xn_fm{b}", [128, c.DC * c.S], BF16, pb_)
                    wbuf = sb(f"wbuf{b}", [128, c.DC * c.D], BF16, pb_)
                    xn_tm = sb(f"xn_tm{b}", [128, 2 * c.D], BF16, pb_)

                    for i in range(c.NTB):
                        xk = f"x{i % 4}"
                        xs = xstr[:, (i % 4) * c.D:(i % 4 + 1) * c.D]
                        U("sy", I("dma_start", xs, x_tile(b * c.NTB + i)),
                          writes=[xk], dma=True)
                        xnsl = xn_tm[:, (i % 2) * c.D:(i % 2 + 1) * c.D]
                        ln_chain(i, xk, xs, xnsl, f"xn_tm{i % 2}")
                        if (dbg_lv in (2, 9)) and b == 0 and i == 0:
                            U("sy", I("dma_start", dbg_t["dbg_stat"][:, 0:8],
                                      stat[:, 0:8]),
                              reads=["st0s0", "st0s1", "st0m", "st0m2", "st0v",
                                     "st0ln", "st0r", "st0mb"], dma=True)
                            U("sy", I("dma_start", dbg_t["dbg_xntm"][:, 0:c.D], xnsl),
                              reads=["xn_tm0"], dma=True)
                        for j in range(c.DC):
                            gidx = i * c.DC + j
                            pt = ptpA if gidx % 2 == 0 else ptpB
                            ptk = f"ptp{gidx % 2}"
                            U("pe", I("transpose", pt[:, :],
                                      xnsl[:, j * 128:(j + 1) * 128], ident[:, :]),
                              reads=[f"xn_tm{i % 2}", "ident"], writes=[ptk])
                            U("se", I("activation",
                                      xn_fm[:, j * c.S + i * 128:j * c.S + (i + 1) * 128],
                                      pt[:, :], AF.Identity, bias=0.0, scale=1.0),
                              reads=[ptk], writes=["xn_fm"])

                    with nc.psum_tensor(f"ppA{b}", [128, 512], F32) as ppA, \
                         nc.psum_tensor(f"ppB{b}", [128, 512], F32) as ppB, \
                         nc.psum_tensor(f"pvA{b}", [128, 512], F32) as pvA, \
                         nc.psum_tensor(f"pvB{b}", [128, 512], F32) as pvB:
                        NS5 = c.S // 512

                        def wload(wd):
                            U("gq", I("dma_start",
                                      wbuf[:, :].rearrange("p (cc f) -> p cc f", cc=c.DC),
                                      wd.rearrange("(cc p) f -> p cc f", p=128)),
                              writes=["wbuf"], dma=True)

                        def proj_qk(dstbuf, dstkey, biasap, scale):
                            cnt = 0
                            for m in range(c.DC):
                                for t5 in range(NS5):
                                    pp = ppA if cnt % 2 == 0 else ppB
                                    ppk = f"pp{cnt % 2}"
                                    g = []
                                    for j in range(c.DC):
                                        g.append(("matmul", (
                                            pp[:, :],
                                            wbuf[:, j * c.D + m * 128:j * c.D + (m + 1) * 128],
                                            xn_fm[:, j * c.S + t5 * 512:j * c.S + (t5 + 1) * 512]),
                                            dict(start=(j == 0), stop=(j == c.DC - 1))))
                                    U("pe", SEQ(g), reads=["wbuf", "xn_fm"], writes=[ppk])
                                    U("ve", I("tensor_scalar",
                                              dstbuf[:, m * c.S + t5 * 512:m * c.S + (t5 + 1) * 512],
                                              pp[:, :],
                                              biasap[:, m:m + 1], scale,
                                              op0=ALU.add, op1=ALU.mult),
                                      reads=[ppk, "bqc", "bkc"], writes=[dstkey])
                                    cnt += 1

                        wload(wq_d)
                        proj_qk(q_fm, "q_fm", bqc, 0.125)
                        wload(wk_d)
                        proj_qk(k_fm, "k_fm", bkc, 1.0)
                        wload(wv_d)
                        U("ve", I("memset",
                                  v_tm[:, :].rearrange("p (i h ee) -> p i h ee",
                                                       i=c.NTB, h=c.H)[:, :, :, 64:65],
                                  1.0), writes=["v_ones"])
                        hh = 512 // 64
                        for i in range(c.NTB):
                            for half in range(c.D // 512):
                                pv = pvA if half % 2 == 0 else pvB
                                pvk = f"pv{half % 2}"
                                g = []
                                for j in range(c.DC):
                                    g.append(("matmul", (
                                        pv[:, :],
                                        xn_fm[:, j * c.S + i * 128:j * c.S + (i + 1) * 128],
                                        wbuf[:, j * c.D + half * 512:j * c.D + (half + 1) * 512]),
                                        dict(start=(j == 0), stop=(j == c.DC - 1))))
                                U("pe", SEQ(g), reads=["wbuf", "xn_fm"], writes=[pvk])
                                vv = v_tm[:, i * c.H * 65 + half * hh * 65:
                                          i * c.H * 65 + (half + 1) * hh * 65]
                                vv = vv.rearrange("p (h ee) -> p h ee", h=hh)[:, :, 0:64]
                                U("ve", I("tensor_tensor", vv,
                                          pv[:, :].rearrange("p (h ee) -> p h ee", h=hh),
                                          bv_bc[:, half * 512:(half + 1) * 512].rearrange(
                                              "p (h ee) -> p h ee", h=hh),
                                          op=ALU.add),
                                  reads=[pvk, "brvbc"], writes=[f"v{i}"])
                        if (dbg_lv in (3, 9)) and b == 0 and i == c.NTB - 1 and half == c.D // 512 - 1:
                            U("sy", I("dma_start", dbg_t["dbg_xn"], xn_fm[:, :]),
                              reads=["xn_fm"], dma=True)
                        if dbg_lv == 7 and b == 0 and i == c.NTB - 1 and half == c.D // 512 - 1:
                            U("gq", I("dma_start", dbg_t["dbg_xn"], xn_fm[:, :]),
                              reads=["xn_fm"], dma=True)
                            U("gq", I("dma_start", dbg_t["dbg_xntm"][:, 0:2 * c.D],
                                      xn_tm[:, :]),
                              reads=["xn_tm0", "xn_tm1"], dma=True)
                            U("gq", I("dma_start", dbg_t["dbg_stat"], stat[:, :]),
                              reads=["st0r", "st1r", "st0mb", "st1mb",
                                     "st0s0", "st0s1", "st0m", "st0m2", "st0v", "st0ln",
                                     "st1s0", "st1s1", "st1m", "st1m2", "st1v", "st1ln"],
                              dma=True)
                    P.barrier()

                if (dbg_lv in (4, 9)) and b == 0:
                    U("sy", I("dma_start", dbg_t["dbg_q"], q_fm[:, :]),
                      reads=["q_fm"], dma=True)
                    U("sy", I("dma_start", dbg_t["dbg_k"], k_fm[:, :]),
                      reads=["k_fm"], dma=True)
                    U("sy", I("dma_start", dbg_t["dbg_v"], v_tm[:, :]),
                      reads=[f"v{i}" for i in range(c.NTB)] + ["v_ones"], dma=True)

                ao_fm = sb(f"ao_fm{b}", [128, c.DC * c.S], BF16, ab)
                hd_ = ExitStack()
                pbuf = sb(f"pbuf{b}", [128, 4 * 512], BF16, hd_)
                rzsb = sb(f"rzsb{b}", [64, 512], F32, hd_)
                rzbf = sb(f"rzbf{b}", [1, 512], BF16, hd_)
                mhat = sb(f"mhat{b}", [1, c.S], BF16, hd_)

                nsq = sb(f"nsq{b}", [64, c.S], BF16, hd_)
                kb2 = sb(f"kb2{b}", [1, 2 * c.H], F32, hd_)

                with nc.psum_tensor(f"pkn{b}", [128, c.S], F32) as pkn:
                    for h in range(c.H):
                        ch, off = h // 2, (h % 2) * 64
                        ks = k_fm[off:off + 64, ch * c.S:(ch + 1) * c.S]
                        U("ve", I("tensor_tensor", nsq[0:64, :], ks, ks, op=ALU.mult),
                          reads=["k_fm"], writes=["nsq"])
                        g = []
                        for w5 in range(c.S // 512):
                            g.append(("matmul", (pkn[:, w5 * 512:(w5 + 1) * 512],
                                                 onesm[0:64, :],
                                                 nsq[0:64, w5 * 512:(w5 + 1) * 512]),
                                      dict(start=True, stop=True)))
                        U("pe", SEQ(g, inc_idx=list(range(len(g)))),
                          reads=["nsq", "onesm"], writes=["pkn"], nincs=len(g))
                        U("ve", I("reduce_max", kb2[0:1, h:h + 1], pkn[0:1, :],
                                  axis=AX.X), reads=["pkn"], writes=[f"kb2_{h}"])
                    U("se", I("activation", kb2[0:1, 0:c.H], kb2[0:1, 0:c.H], AF.Ln,
                              bias=0.0, scale=1.0),
                      reads=[f"kb2_{h}" for h in range(c.H)], writes=["kb2"])
                    U("ve", I("tensor_scalar", kb2[0:1, 0:c.H], kb2[0:1, 0:c.H],
                              0.5, 0.02, op0=ALU.mult, op1=ALU.add),
                      reads=["kb2"], writes=["kb2"])
                    P.barrier()

                qsums = sb(f"qsums{b}", [1, 2 * c.S], F32, hd_)
                mhat2 = sb(f"mhat2{b}", [1, c.S], BF16, hd_)
                nsq2 = sb(f"nsq2{b}", [64, c.S], BF16, hd_)
                with nc.psum_tensor(f"pqn{b}", [128, 512], F32) as pqn, \
                     nc.psum_tensor(f"psA{b}", [128, 512], F32) as psA, \
                     nc.psum_tensor(f"psB{b}", [128, 512], F32) as psB, \
                     nc.psum_tensor(f"psC{b}", [128, 512], F32) as psC, \
                     nc.psum_tensor(f"psD{b}", [128, 512], F32) as psD, \
                     nc.psum_tensor(f"ppv{b}", [65, 512], F32) as ppv, \
                     nc.psum_tensor(f"ppv2{b}", [65, 512], F32) as ppv2, \
                     nc.psum_tensor(f"pbcz{b}", [64, 512], F32) as pbcz:
                    PS = [psA, psB, psC, psD]
                    PPV = [ppv, ppv2]

                    pending_rz = []

                    def flush_rz():
                        while pending_rz:
                            pending_rz.pop(0)()

                    def mhat_chain(h):
                        par = h % 2
                        ch, off = h // 2, (h % 2) * 64
                        nsq_ = nsq if par == 0 else nsq2
                        mh_ = mhat if par == 0 else mhat2
                        qs = q_fm[off:off + 64, ch * c.S:(ch + 1) * c.S]
                        U("ve", I("tensor_tensor", nsq_[0:64, :], qs, qs, op=ALU.mult),
                          reads=["q_fm"], writes=[f"nsq{par}"])
                        for w5 in range(c.S // 512):
                            U("pe", I("matmul", pqn[:, :], onesm[0:64, :],
                                      nsq_[0:64, w5 * 512:(w5 + 1) * 512],
                                      start=True, stop=True),
                              reads=[f"nsq{par}", "onesm"], writes=["pqn"])
                            U("ve", I("tensor_copy",
                                      qsums[0:1, par * c.S + w5 * 512:par * c.S + (w5 + 1) * 512],
                                      pqn[0:1, :]),
                              reads=["pqn"], writes=[f"qsum{par}_{w5}", f"qln{par}"])
                        U("se", I("activation", qsums[0:1, par * c.S:(par + 1) * c.S],
                                  qsums[0:1, par * c.S:(par + 1) * c.S], AF.Ln,
                                  bias=0.0, scale=1.0),
                          reads=[f"qsum{par}_{w5}" for w5 in range(c.S // 512)],
                          writes=[f"qln{par}"])
                        U("se", I("activation", mh_[0:1, :],
                                  qsums[0:1, par * c.S:(par + 1) * c.S], AF.Exp,
                                  bias=kb2[0:1, h:h + 1], scale=0.5),
                          reads=[f"qln{par}", "kb2"], writes=[f"mhat{par}"])

                    for h in range(c.H):
                        ch, off = h // 2, (h % 2) * 64
                        if h == 0:
                            mhat_chain(0)
                        if h + 1 < c.H:
                            mhat_chain(h + 1)
                        mh_cur = mhat if h % 2 == 0 else mhat2

                        for w in range(c.NW):
                            nch = (w + 1) * (c.TW // 128)
                            diag0 = nch - (c.TW // 128)

                            ppv_c = PPV[w % 2]
                            ppk = f"ppv{w % 2}"

                            def emit_pv(a, h=h, nch=nch, ppv_c=ppv_c, ppk=ppk):
                                U("pe", I("matmul", ppv_c[:, :],
                                          v_tm[:, a * c.H * 65 + h * 65:
                                               a * c.H * 65 + (h + 1) * 65],
                                          pbuf[:, (a % 4) * 512:(a % 4 + 1) * 512],
                                          start=(a == 0), stop=(a == nch - 1)),
                                  reads=[f"P{a % 4}", f"v{a}", "v_ones"],
                                  writes=[ppk])

                            LAG = 3
                            for a in range(nch):
                                ps = PS[a % 4]
                                psk = f"ps{a % 4}"
                                g = [("matmul", (
                                    ps[:, :],
                                    k_fm[off:off + 64, ch * c.S + a * 128:ch * c.S + (a + 1) * 128],
                                    q_fm[off:off + 64, ch * c.S + w * 512:ch * c.S + (w + 1) * 512]),
                                    dict(start=True, stop=False)),
                                    ("matmul", (ps[:, :], negob[0:1, :],
                                                mh_cur[0:1, w * 512:(w + 1) * 512]),
                                     dict(start=False, stop=True))]
                                U("pe", SEQ(g),
                                  reads=["k_fm", "q_fm", f"mhat{h % 2}", "negob"],
                                  writes=[psk])
                                U("se", I("activation",
                                          pbuf[:, (a % 4) * 512:(a % 4 + 1) * 512],
                                          ps[:, :], AF.Exp, bias=0.0, scale=1.0),
                                  reads=[psk], writes=[f"P{a % 4}"])
                                if a >= diag0:
                                    delta = a * 128 - w * 512
                                    U("ve", I("tensor_tensor",
                                              pbuf[:, (a % 4) * 512:(a % 4 + 1) * 512],
                                              pbuf[:, (a % 4) * 512:(a % 4 + 1) * 512],
                                              maskb[:, 384 - delta:384 - delta + 512],
                                              op=ALU.mult),
                                      reads=[f"P{a % 4}", "maskb"], writes=[f"P{a % 4}"])
                                if a >= LAG:
                                    emit_pv(a - LAG)
                                if a == 1:
                                    flush_rz()
                            for a2 in range(max(0, nch - LAG), nch):
                                emit_pv(a2)

                            def rz_tail(h=h, w=w, ch=ch, off=off,
                                        ppv_c=ppv_c, ppk=ppk):
                                U("ve", I("reciprocal", rzsb[0:1, :],
                                          ppv_c[64:65, :]),
                                  reads=[ppk], writes=["rz"])
                                U("ve", I("tensor_copy", rzbf[0:1, :],
                                          rzsb[0:1, :]),
                                  reads=["rz"], writes=["rzbf"])
                                U("pe", I("matmul", pbcz[:, :], onesb[0:1, 0:64],
                                          rzbf[0:1, :], start=True, stop=True),
                                  reads=["rzbf", "onesb"], writes=["pbcz"])
                                U("ve", I("tensor_copy", rzsb[0:64, :],
                                          pbcz[:, :]),
                                  reads=["pbcz"], writes=["rzb64"])
                                U("ve", I("tensor_tensor",
                                          ao_fm[off:off + 64,
                                                ch * c.S + w * 512:ch * c.S + (w + 1) * 512],
                                          ppv_c[0:64, :], rzsb[0:64, :],
                                          op=ALU.mult),
                                  reads=[ppk, "rzb64"], writes=["ao_fm"])
                            pending_rz.append(rz_tail)
                    flush_rz()
                    if (dbg_lv in (6, 7)) and b == 0:
                        U("gq", I("dma_start", dbg_t["dbg_mhat"], mhat[0:1, :]),
                          reads=["mhat"], dma=True)
                        U("gq", I("dma_start", dbg_t["dbg_kb2"], kb2[0:1, 0:c.H]),
                          reads=["kb2"], dma=True)
                        U("gq", I("dma_start", dbg_t["dbg_ao"], ao_fm[:, :]),
                          reads=["ao_fm"], dma=True)
                        U("gq", I("dma_start", dbg_t["dbg_q"], q_fm[:, :]),
                          reads=["q_fm"], dma=True)
                        U("gq", I("dma_start", dbg_t["dbg_k"], k_fm[:, :]),
                          reads=["k_fm"], dma=True)
                        U("gq", I("dma_start", dbg_t["dbg_v"], v_tm[:, :]),
                          reads=[f"v{i}" for i in range(c.NTB)] + ["v_ones"], dma=True)
                    if (dbg_lv in (5, 9)) and b == 0:
                        U("sy", I("dma_start", dbg_t["dbg_mhat"], mhat[0:1, :]),
                          reads=["mhat"], dma=True)
                        U("sy", I("dma_start", dbg_t["dbg_kb2"], kb2[0:1, 0:c.H]),
                          reads=["kb2"], dma=True)
                        U("sy", I("dma_start", dbg_t["dbg_ao"], ao_fm[:, :]),
                          reads=["ao_fm"], dma=True)
                    P.barrier()

                hd_.close()

                with ExitStack() as eb, \
                     nc.psum_tensor(f"pdA0{b}", [128, 512], F32) as pdA0, \
                     nc.psum_tensor(f"pdA1{b}", [128, 512], F32) as pdA1, \
                     nc.psum_tensor(f"pdB0{b}", [128, 512], F32) as pdB0, \
                     nc.psum_tensor(f"pdB1{b}", [128, 512], F32) as pdB1, \
                     nc.psum_tensor(f"ptpA2{b}", [128, 128], BF16) as ptpA2, \
                     nc.psum_tensor(f"ptpB2{b}", [128, 128], BF16) as ptpB2, \
                     nc.psum_tensor(f"plg{b}", [128, c.E], F32) as plg:

                    wob = sb(f"wob{b}", [128, c.DC * c.D], BF16, eb)
                    x1sb = sb(f"x1sb{b}", [128, 2 * c.D], F32, eb)
                    xn1t = sb(f"xn1t{b}", [128, 2 * c.D], BF16, eb)
                    stg = sb(f"stg{b}", [128, c.DC * 512], BF16, eb)
                    U("gq", I("dma_start",
                              wob[:, :].rearrange("p (cc f) -> p cc f", cc=c.DC),
                              wo_d.rearrange("(cc p) f -> p cc f", p=128)),
                      writes=["wob"], dma=True)

                    for i in range(c.NTB):
                        gt = b * c.NTB + i
                        xk = f"ex{i % 4}"
                        xs = xstr[:, (i % 4) * c.D:(i % 4 + 1) * c.D]
                        U("sy", I("dma_start", xs, x_tile(gt)), writes=[xk], dma=True)
                        NH = c.D // 512
                        pds = [pdA0, pdA1] if i % 2 == 0 else [pdB0, pdB1]
                        pdk = [f"pd{i % 2}0", f"pd{i % 2}1"][:NH]
                        g = []
                        for j in range(c.DC):
                            lhs = ao_fm[:, j * c.S + i * 128:j * c.S + (i + 1) * 128]
                            for half in range(NH):
                                g.append(("matmul", (pds[half][:, :], lhs,
                                                     wob[:, j * c.D + half * 512:j * c.D + (half + 1) * 512]),
                                          dict(start=(j == 0), stop=(j == c.DC - 1))))
                        U("pe", SEQ(g, inc_idx=list(range(len(g) - NH, len(g)))),
                          reads=["ao_fm", "wob"], writes=pdk, nincs=NH)

                        x1sl = x1sb[:, (i % 2) * c.D:(i % 2 + 1) * c.D]
                        x1k = f"x1_{i % 2}"
                        hkeys = []
                        for half in range(NH):
                            hk = x1k + f"h{half}"
                            hkeys.append(hk)
                            U("ve", I("scalar_tensor_tensor",
                                      x1sl[:, half * 512:(half + 1) * 512],
                                      pds[half][:, :], 1.0,
                                      bo_bc[:, half * 512:(half + 1) * 512],
                                      op0=ALU.mult, op1=ALU.add),
                              reads=[pdk[half], "brobc"], writes=[hk, x1k])
                        U("ve", I("tensor_tensor", x1sl, x1sl, xs, op=ALU.add),
                          reads=hkeys + [xk], writes=[x1k])
                        xnsl = xn1t[:, (i % 2) * c.D:(i % 2 + 1) * c.D]
                        ln_chain(i, xk, x1sl, xnsl, f"xn1t{i % 2}", x1key=x1k)
                        U("sy", I("dma_start", x1_d[gt * 128:(gt + 1) * 128, :], x1sl),
                          reads=[x1k], dma=True)
                        for j in range(c.DC):
                            gidx = i * c.DC + j
                            pt = ptpA2 if gidx % 2 == 0 else ptpB2
                            ptk = f"ptp2_{gidx % 2}"
                            U("pe", I("transpose", pt[:, :],
                                      xnsl[:, j * 128:(j + 1) * 128], ident[:, :]),
                              reads=[f"xn1t{i % 2}", "ident"], writes=[ptk])
                            U("se", I("activation",
                                      stg[:, j * 512 + (i % 4) * 128:j * 512 + (i % 4 + 1) * 128],
                                      pt[:, :], AF.Identity, bias=0.0, scale=1.0),
                              reads=[ptk], writes=[f"stg{i % 4}_{j}"])
                        g = []
                        for j in range(c.DC):
                            g.append(("matmul", (
                                plg[:, :],
                                stg[:, j * 512 + (i % 4) * 128:j * 512 + (i % 4 + 1) * 128],
                                wg8[:, j * c.E:(j + 1) * c.E]),
                                dict(start=(j == 0), stop=(j == c.DC - 1))))
                        U("pe", SEQ(g),
                          reads=[f"stg{i % 4}_{j}" for j in range(c.DC)] + ["wg8"],
                          writes=["plg"])
                        U("ve", I("tensor_copy", lgs[:, 0:c.E], plg[:, :]),
                          reads=["plg"], writes=["lg0"])
                        U("ve", I("reduce_max", gstat[:, 0:1], lgs[:, 0:c.E], axis=AX.X),
                          reads=["lg0"], writes=["gm1"])
                        U("ve", I("tensor_scalar", lgs[:, c.E:2 * c.E], lgs[:, 0:c.E],
                                  gstat[:, 0:1], None, op0=ALU.is_ge),
                          reads=["lg0", "gm1"], writes=["lg1"])
                        U("ve", I("scalar_tensor_tensor", lgs[:, c.E:2 * c.E],
                                  lgs[:, c.E:2 * c.E], -1e9, lgs[:, 0:c.E],
                                  op0=ALU.mult, op1=ALU.add),
                          reads=["lg1", "lg0"], writes=["lg1"])
                        U("ve", I("reduce_max", gstat[:, 1:2], lgs[:, c.E:2 * c.E],
                                  axis=AX.X), reads=["lg1"], writes=["gm2"])
                        U("ve", I("tensor_scalar", lgs[:, c.E:2 * c.E], lgs[:, 0:c.E],
                                  gstat[:, 1:2], None, op0=ALU.is_ge),
                          reads=["lg0", "gm2"], writes=["lg1"])
                        U("ve", I("tensor_scalar", gstat[:, 2:3], gstat[:, 0:1], -1.0,
                                  None, op0=ALU.mult), reads=["gm1"], writes=["gnm"])
                        U("se", I("activation", lgs[:, 0:c.E], lgs[:, 0:c.E], AF.Exp,
                                  bias=gstat[:, 2:3], scale=1.0),
                          reads=["lg0", "gnm"], writes=["lg0"])
                        U("ve", I("tensor_tensor", lgs[:, 0:c.E], lgs[:, 0:c.E],
                                  lgs[:, c.E:2 * c.E], op=ALU.mult),
                          reads=["lg0", "lg1"], writes=["lg0"])
                        U("ve", I("reduce_sum", gstat[:, 3:4], lgs[:, 0:c.E], axis=AX.X),
                          reads=["lg0"], writes=["gz"])
                        U("ve", I("reciprocal", gstat[:, 4:5], gstat[:, 3:4]),
                          reads=["gz"], writes=["grz"])
                        U("ve", I("tensor_scalar", gates[:, gt * c.E:(gt + 1) * c.E],
                                  lgs[:, 0:c.E], gstat[:, 4:5], None, op0=ALU.mult),
                          reads=["lg0", "grz"], writes=[f"gates{gt}"])
                        if i % 4 == 3:
                            base_t = b * c.S + (i - 3) * 128
                            U("sy", I("dma_start",
                                      xn1_d[:, :, base_t:base_t + 512].rearrange(
                                          "cc p f -> p cc f"),
                                      stg[:, :].rearrange("p (cc f) -> p cc f", cc=c.DC)),
                              reads=[f"stg{k}_{j}" for k in range(4)
                                     for j in range(c.DC)],
                              dma=True)
                    P.barrier()

        attn_es.close()

        with ExitStack() as fb, \
             nc.psum_tensor("q0", [128, 512], F32) as q0, \
             nc.psum_tensor("q1", [128, 512], F32) as q1, \
             nc.psum_tensor("q2", [128, 512], F32) as q2, \
             nc.psum_tensor("q3", [128, 512], F32) as q3, \
             nc.psum_tensor("q4", [128, 512], F32) as q4, \
             nc.psum_tensor("q5", [128, 512], F32) as q5, \
             nc.psum_tensor("q6", [128, 512], F32) as q6, \
             nc.psum_tensor("q7", [128, 512], F32) as q7:

            QQ = [q0, q1, q2, q3, q4, q5, q6, q7]
            wd_sb = sb("wd_sb", [128, c.FC * c.D], BF16, fb)
            NSLOT = 2
            wg_sb = sb("wg_sb", [128, NSLOT * c.DC * 256], BF16, fb)
            wu_sb = sb("wu_sb", [128, NSLOT * c.DC * 256], BF16, fb)
            act_sb = sb("act_sb", [128, c.FC * c.TR], BF16, fb)
            oacc = sb("oacc", [128, c.TRT * c.D], F32, fb)
            xn1q = sb("xn1q", [128, c.DC * c.TR], BF16, fb)
            sg_sb = sb("sg_sb", [128, 2 * c.TR], BF16, fb)

            U("se", I("nop"), writes=["acttbl"])

            wgu_r = [wgu_d[en].rearrange("(cc p) f -> p cc f", p=128)
                     for en in range(c.E)]
            wdn_r = [wdn_d[en].rearrange("(cc p) f -> p cc f", p=128)
                     for en in range(c.E)]
            NBLK = c.F // 256

            for p in range(c.NP):
                U("gq", I("dma_start",
                          xn1q[:, :].rearrange("p (cc tt) -> p cc tt", cc=c.DC),
                          xn1_d[:, :, p * c.TR:(p + 1) * c.TR].rearrange(
                              "cc p tt -> p cc tt")),
                  writes=["xn1q"], dma=True)
                U("sy", I("dma_start",
                          oacc[:, :].rearrange("p (i d) -> p i d", i=c.TRT),
                          x1_d[p * c.TR:(p + 1) * c.TR, :].rearrange(
                              "(i p) d -> p i d", p=128)),
                  writes=[f"oacc{t}" for t in range(c.TRT)], dma=True)

                for en in range(c.E):
                    Eg = p * c.E + en

                    def wgu_block(j, Eg=Eg, en=en):
                        slot = (Eg * NBLK + j) % NSLOT
                        for nm, buf, off in (("wg", wg_sb, 0), ("wu", wu_sb, c.F)):
                            U("gq", I("dma_start",
                                      buf[:, slot * c.DC * 256:(slot + 1) * c.DC * 256].rearrange(
                                          "p (cc f) -> p cc f", cc=c.DC),
                                      wgu_r[en][:, :, off + j * 256:off + (j + 1) * 256]),
                              writes=[f"{nm}{slot}"], dma=True)

                    def wd_quarter(qtr, en=en):
                        lo, hi = qtr * (c.FC // 4), (qtr + 1) * (c.FC // 4)
                        U("gq", I("dma_start",
                                  wd_sb[:, lo * c.D:hi * c.D].rearrange(
                                      "p (cc f) -> p cc f", cc=hi - lo),
                                  wdn_r[en][:, lo:hi, :]),
                          writes=[f"wd{qtr}"], dma=True)

                    wd_mark = [P.w[f"wd{q}"][1] if f"wd{q}" in P.w else -1
                               for q in range(4)]
                    npre = min(NSLOT, NBLK)
                    for j in range(npre):
                        wgu_block(j)

                    for fc in range(c.FC):
                        jb, sub = fc // 2, fc % 2
                        slot = (Eg * NBLK + jb) % NSLOT
                        par = fc % 2
                        gq_ = [QQ[par * 4], QQ[par * 4 + 1]]
                        uq_ = [QQ[par * 4 + 2], QQ[par * 4 + 3]]
                        gk = [f"q{par * 4}", f"q{par * 4 + 1}"]
                        uk = [f"q{par * 4 + 2}", f"q{par * 4 + 3}"]
                        base = slot * c.DC * 256 + sub * 128
                        g = []
                        for wbuf_, qq_ in ((wg_sb, gq_), (wu_sb, uq_)):
                            for j in range(c.DC):
                                lhs = wbuf_[:, base + j * 256:base + j * 256 + 128]
                                for t5 in range(c.T5):
                                    g.append(("matmul", (
                                        qq_[t5][:, :], lhs,
                                        xn1q[:, j * c.TR + t5 * 512:j * c.TR + (t5 + 1) * 512]),
                                        dict(start=(j == 0), stop=(j == c.DC - 1))))
                        half = len(g) // 2
                        U("pe", SEQ(g, inc_idx=list(range(half - c.T5, half)) +
                                    list(range(len(g) - c.T5, len(g)))),
                          reads=[f"wg{slot}", f"wu{slot}", "xn1q"],
                          writes=gk[:c.T5] + uk[:c.T5], nincs=2 * c.T5,
                          defer_drain=(fc < c.FC - 1))
                        for t5 in range(c.T5):
                            sgsl = sg_sb[:, par * c.TR + t5 * 512:par * c.TR + (t5 + 1) * 512]
                            U("se", I("activation", sgsl, gq_[t5][:, :], AF.Silu,
                                      bias=0.0, scale=1.0),
                              reads=[gk[t5], "acttbl"], writes=[f"sg{par}_{t5}"])
                            U("ve", I("tensor_tensor",
                                      act_sb[:, fc * c.TR + t5 * 512:fc * c.TR + (t5 + 1) * 512],
                                      sgsl, uq_[t5][:, :], op=ALU.mult),
                              reads=[f"sg{par}_{t5}", uk[t5]], writes=[f"act{fc}"])
                        if sub == 1 and jb + npre < NBLK:
                            wgu_block(jb + npre)
                        if sub == 1 and jb % max(1, NBLK // 4) == 1 and jb // max(1, NBLK // 4) < 4:
                            wd_quarter(jb // max(1, NBLK // 4))

                    for qtr in range((c.FC and 0) or 0, 4):
                        if f"wd{qtr}" not in P.w or P.w[f"wd{qtr}"][1] <= wd_mark[qtr]:
                            wd_quarter(qtr)
                    NSW = c.TRT // 2
                    for s in range(NSW):
                        par = s % 2
                        banks = [QQ[par * 4 + k] for k in range(4)]
                        bkeys = [f"q{par * 4 + k}" for k in range(4)]
                        NH = c.D // 512
                        g = []
                        for fc in range(c.FC):
                            for ti in range(2):
                                tile = s * 2 + ti
                                lhs = act_sb[:, fc * c.TR + tile * 128:fc * c.TR + (tile + 1) * 128]
                                for half in range(NH):
                                    g.append(("matmul", (
                                        banks[ti * 2 + half][:, :], lhs,
                                        wd_sb[:, fc * c.D + half * 512:fc * c.D + (half + 1) * 512]),
                                        dict(start=(fc == 0), stop=(fc == c.FC - 1))))
                        nch_ = 2 * NH
                        U("pe", SEQ(g, inc_idx=list(range(len(g) - nch_, len(g)))),
                          reads=[f"act{fc}" for fc in range(c.FC)] +
                                [f"wd{qtr}" for qtr in range(4)],
                          writes=bkeys, nincs=nch_,
                          defer_drain=(s < NSW - 1))
                        for ti in range(2):
                            tile = s * 2 + ti
                            gt = p * c.TRT + tile
                            for half in range(NH):
                                osl = oacc[:, tile * c.D + half * 512:tile * c.D + (half + 1) * 512]
                                U("ve", I("scalar_tensor_tensor", osl,
                                          banks[ti * 2 + half][:, :],
                                          gates[:, gt * c.E + en:gt * c.E + en + 1],
                                          osl, op0=ALU.mult, op1=ALU.add),
                                  reads=[bkeys[ti * 2 + half], f"gates{gt}",
                                         f"oacc{tile}"],
                                  writes=[f"oacc{tile}"])

                t0 = p * c.TR
                bb, rr = t0 // c.S, t0 % c.S
                U("sy", I("dma_start",
                          out_d[bb, rr:rr + c.TR, :].rearrange("(i p) d -> p i d", p=128),
                          oacc[:, :].rearrange("p (i d) -> p i d", i=c.TRT)),
                  reads=[f"oacc{t}" for t in range(c.TRT)],
                  writes=[f"outdma{p}"], dma=True)

            if dbg:
                U("sy", I("dma_start", dbg_t["dbg_gates"], gates[:, :]),
                  reads=[f"gates{gt}" for gt in range(c.NT)], dma=True)
                U("sy", I("dma_start", dbg_t["dbg_act"], act_sb[:, :]),
                  reads=[f"act{fc}" for fc in range(c.FC)], dma=True)
            U("gp", I("nop"), reads=[f"outdma{p}" for p in range(c.NP)])

        blk = es.enter_context(nc.Block())
        P.emit(blk)

    nc.finalize()
    return nc


_CACHE = {}


def kernel(**inputs):
    cfg = CFG()
    n = cfg.n_cores
    x = np.ascontiguousarray(np.asarray(inputs["L0_attn_x"], dtype=np.float32))
    bpc = x.shape[0] // n
    weights = {k: np.ascontiguousarray(np.asarray(v, dtype=np.float32))
               for k, v in inputs.items() if k != "L0_attn_x"}
    in_maps = []
    for ci in range(n):
        m = {"L0_attn_x": x[ci * bpc:(ci + 1) * bpc]}
        m.update(weights)
        in_maps.append(m)
    if "nc" not in _CACHE:
        _CACHE["nc"] = build_program(cfg)
    res = run_bass_kernel_spmd(_CACHE["nc"], in_maps, core_ids=list(range(n)))
    return np.concatenate([r["out"] for r in res.results], axis=0)


# version 10
# speedup vs baseline: 1.0127x; 1.0127x over previous
import numpy as np
import ml_dtypes
from contextlib import ExitStack

import concourse.bass as bass
import concourse.mybir as mybir
from concourse.bass_utils import run_bass_kernel_spmd

F32 = mybir.dt.float32
BF16 = mybir.dt.bfloat16
AF = mybir.ActivationFunctionType
ALU = mybir.AluOpType
AX = mybir.AxisListType


class CFG:
    def __init__(self, B=2, S=2048, D=1024, F=4096, E=8, TR=1024, n_cores=8):
        self.B, self.S, self.D, self.F, self.E, self.TR = B, S, D, F, E, TR
        self.n_cores = n_cores
        self.DK = 64
        self.H = D // 64
        self.DC = D // 128
        self.FC = F // 128
        self.TW = 512
        self.NW = S // self.TW
        self.T = B * S
        self.NT = self.T // 128
        self.NP = self.T // TR
        self.TRT = TR // 128
        self.T5 = TR // 512
        self.NTB = S // 128
        assert D % 128 == 0 and F % 512 == 0 and S % 512 == 0 and TR % 512 == 0
        assert self.T % TR == 0 and self.S % TR == 0


NCH = 8
ENG = (("pe", "ve", "se", "gp")
       + tuple(f"syc{k}" for k in range(NCH))
       + tuple(f"gqc{k}" for k in range(NCH)))
PHYS = {"pe": "tensor", "ve": "vector", "se": "scalar", "gp": "gpsimd"}
for _k in range(NCH):
    PHYS[f"syc{_k}"] = "sync"
    PHYS[f"gqc{_k}"] = "gpsimd"


def I(method, *args, **kw):
    def f(e):
        return [getattr(e, method)(*args, **kw)]
    return f


def SEQ(instrs, inc_idx=None):
    if inc_idx is None:
        inc_idx = [len(instrs) - 1]
    inc_set = set(i % len(instrs) for i in inc_idx)

    def f(e):
        out = []
        for n, (m, a, kw) in enumerate(instrs):
            ins = getattr(e, m)(*a, **kw)
            if n in inc_set:
                out.append(ins)
        return out
    return f


class Plan:
    def __init__(self, nc, es):
        self.nc = nc
        self.units = []
        self.idx = {e: 0 for e in ENG}
        self.w = {}
        self.r = {}
        self.sems = {e: es.enter_context(nc.semaphore(f"sem_{e}")) for e in ENG}
        self.floors = {e: {} for e in ENG}
        self.dmacnt = {"sy": 0, "gq": 0}
        self.pe_defer = None
        self.defer_map = {}

    def barrier(self):
        snap = dict(self.idx)
        for e in ENG:
            f = self.floors[e]
            for e2 in ENG:
                if e2 != e and snap[e2] > 0:
                    f[e2] = max(f.get(e2, 0), snap[e2])

    def unit(self, eng, fn, reads=(), writes=(), dma=False, nincs=1,
             defer_drain=False):
        if dma:
            ch = self.dmacnt[eng] % NCH
            self.dmacnt[eng] += 1
            eng = f"{eng}c{ch}"
            writes = list(writes) + [f"_{eng}"]
        need = {}

        def _need(e2, i2):
            if e2 == "pe" and eng == "pe":
                return
            if i2 > need.get(e2, 0):
                need[e2] = i2

        for k in reads:
            p = self.w.get(k)
            if p:
                _need(*p)
        for k in writes:
            p = self.w.get(k)
            if p:
                _need(*p)
            for e2, i2 in self.r.get(k, {}).items():
                _need(e2, i2)
        for e2, i2 in self.floors[eng].items():
            if i2 > need.get(e2, 0):
                need[e2] = i2
        self.floors[eng] = {}
        has_drain = eng in ("pe", "ve", "se", "gp") and not defer_drain
        if has_drain:
            nincs += 1
        inc = 16 if dma else 1
        if eng == "pe" and self.pe_defer is not None:
            self.defer_map[self.pe_defer] = self.idx[eng] + 1
            self.pe_defer = None
        self.idx[eng] += inc * nincs
        my = (eng, self.idx[eng])
        if eng == "pe" and defer_drain:
            self.pe_defer = my[1]
        for k in writes:
            self.w[k] = my
            self.r[k] = {}
        for k in reads:
            rr = self.r.setdefault(k, {})
            if rr.get(eng, 0) < my[1]:
                rr[eng] = my[1]
        self.units.append((eng, sorted(need.items()), fn, inc, nincs,
                           has_drain if eng in ("pe", "ve", "se", "gp") else False))

    def emit(self, blk):
        assert self.pe_defer is None, "last PE unit cannot defer its drain"
        streams = {v: [] for v in set(PHYS.values())}
        for u in self.units:
            streams[PHYS[u[0]]].append(u)

        def mk(items):
            def body(e):
                hw = {}
                for eng, waits, fn, inc, nincs, has_drain in items:
                    for e2, v in waits:
                        if e2 == "pe":
                            v = self.defer_map.get(v, v)
                        if hw.get(e2, 0) >= v:
                            continue
                        e.wait_ge(self.sems[e2], v)
                        hw[e2] = v
                    instrs = fn(e)
                    if has_drain:
                        instrs = instrs + [e.drain()]
                    assert len(instrs) == nincs, (eng, len(instrs), nincs)
                    for ins in instrs:
                        ins.then_inc(self.sems[eng], inc)
            return body

        for sname in ("sync", "gpsimd", "tensor", "scalar", "vector"):
            if streams.get(sname):
                getattr(blk, sname)(mk(streams[sname]))


def build_program(cfg: CFG, dbg=False) -> bass.Bass:
    c = cfg
    nc = bass.Bass("TRN2", target_bir_lowering=False)

    x_in = nc.dram_tensor("L0_attn_x", [c.B, c.S, c.D], F32, kind="ExternalInput").ap()
    wq_d = nc.dram_tensor("L0_attn_Wq", [c.D, c.D], F32, kind="ExternalInput").ap()
    bq_d = nc.dram_tensor("L0_attn_bq", [c.D], F32, kind="ExternalInput").ap()
    wk_d = nc.dram_tensor("L0_attn_Wk", [c.D, c.D], F32, kind="ExternalInput").ap()
    bk_d = nc.dram_tensor("L0_attn_bk", [c.D], F32, kind="ExternalInput").ap()
    wv_d = nc.dram_tensor("L0_attn_Wv", [c.D, c.D], F32, kind="ExternalInput").ap()
    bv_d = nc.dram_tensor("L0_attn_bv", [c.D], F32, kind="ExternalInput").ap()
    wo_d = nc.dram_tensor("L0_attn_Wo", [c.D, c.D], F32, kind="ExternalInput").ap()
    bo_d = nc.dram_tensor("L0_attn_bo", [c.D], F32, kind="ExternalInput").ap()
    wg8_d = nc.dram_tensor("L0_ffn_w_gate", [c.D, c.E], F32, kind="ExternalInput").ap()
    wgu_d = nc.dram_tensor("L0_ffn_w_gate_up", [c.E, c.D, 2 * c.F], F32,
                           kind="ExternalInput").ap()
    wdn_d = nc.dram_tensor("L0_ffn_w_down", [c.E, c.F, c.D], F32,
                           kind="ExternalInput").ap()
    out_d = nc.dram_tensor("out", [c.B, c.S, c.D], F32, kind="ExternalOutput").ap()
    scr_kind = "ExternalOutput" if dbg else "Internal"
    dbg_full = dbg is True
    dbg_lv = {False: 0, 'late': 1, 'v1': 2, 'v2': 3, 'v3': 4, 'v4': 5, 'v5': 6, 'v6': 7, True: 9}[dbg]
    x1_d = nc.dram_tensor("x1_scr", [c.T, c.D], F32, kind=scr_kind).ap()
    xn1_d = nc.dram_tensor("xn1_scr", [c.DC, 128, c.T], BF16, kind=scr_kind).ap()
    dbg_t = {}
    if dbg:
        for nm, shape, dt_ in (("dbg_xn", [128, c.DC * c.S], BF16),
                               ("dbg_q", [128, c.DC * c.S], BF16),
                               ("dbg_k", [128, c.DC * c.S], BF16),
                               ("dbg_v", [128, c.NTB * c.H * 65], BF16),
                               ("dbg_ao", [128, c.DC * c.S], BF16),
                               ("dbg_mhat", [1, c.S], BF16),
                               ("dbg_kb2", [1, c.H], F32),
                               ("dbg_gates", [128, c.NT * c.E], F32),
                               ("dbg_act", [128, c.FC * c.TR], BF16),
                               ("dbg_stat", [128, 16], F32),
                               ("dbg_xntm", [128, 2 * 1024], BF16)):
            dbg_t[nm] = nc.dram_tensor(nm, shape, dt_, kind="ExternalOutput").ap()

    ident_dram = nc.inline_tensor(np.eye(128, dtype=ml_dtypes.bfloat16), name="ident_c")

    def x_tile(t):
        b, r = t // c.NTB, (t % c.NTB) * 128
        return x_in[b, r:r + 128, :]

    es = ExitStack()
    with es:
        P = Plan(nc, es)
        U = P.unit

        def sb(name, shape, dtype, stk=es):
            return stk.enter_context(nc.sbuf_tensor(name, shape, dtype))

        ident = sb("identsb", [128, 128], BF16)
        onesb = sb("onesb", [1, 128], BF16)
        negob = sb("negob", [1, 128], BF16)
        onescol = sb("onescol", [128, 1], BF16)
        onesm = sb("onesm", [128, 128], BF16)
        onesfr = sb("onesfr", [1, 128], F32)
        onesf = sb("onesf", [1, 1], F32)
        epsap = sb("epsap", [128, 1], F32)
        maskb = sb("maskb", [128, 896], BF16)
        sqb = sb("sqb", [128, c.D], F32)
        bqc = sb("bqc", [128, c.DC], F32)
        bkc = sb("bkc", [128, c.DC], F32)
        wg8 = sb("wg8", [128, c.DC * c.E], BF16)
        gates = sb("gates", [128, c.NT * c.E], F32)
        stat = sb("stat", [128, 16], F32)
        gstat = sb("gstat", [128, 8], F32)
        lgs = sb("lgs", [128, 2 * c.E], F32)

        attn_es = ExitStack()
        bv_bc = sb("bv_bc", [128, c.D], F32, attn_es)
        bo_bc = sb("bo_bc", [128, c.D], F32, attn_es)
        ph0_es = ExitStack()
        brow = sb("brow", [1, 4 * c.D], F32, ph0_es)
        mwk = sb("mwk", [128, 896], F32, ph0_es)
        U("sy", I("dma_start", ident[:, :], ident_dram.ap()), writes=["ident"], dma=True)
        U("sy", I("dma_start", brow[0:1, 0:c.D], bq_d[None, :]), writes=["brq"], dma=True)
        U("sy", I("dma_start", brow[0:1, c.D:2 * c.D], bk_d[None, :]), writes=["brk"], dma=True)
        U("sy", I("dma_start", brow[0:1, 2 * c.D:3 * c.D], bv_d[None, :]), writes=["brv"], dma=True)
        U("sy", I("dma_start", brow[0:1, 3 * c.D:4 * c.D], bo_d[None, :]), writes=["bro"], dma=True)
        U("gq", I("dma_start", wg8[:, :].rearrange("p (cc e) -> p cc e", cc=c.DC),
                  wg8_d.rearrange("(cc p) e -> p cc e", p=128)), writes=["wg8"], dma=True)
        U("gp", I("iota", mwk[:, :], [[1, 896]], channel_multiplier=-1,
                  allow_small_or_imprecise_dtypes=True), writes=["mwk"])
        U("ve", I("tensor_scalar", mwk[:, :], mwk[:, :], 384.0, None, op0=ALU.is_ge),
          reads=["mwk"], writes=["mwk"])
        U("ve", I("tensor_copy", maskb[:, :], mwk[:, :]), reads=["mwk"], writes=["maskb"])
        U("ve", I("memset", onesb[:, :], 1.0), writes=["onesb"])
        U("ve", I("memset", negob[:, :], -1.0), writes=["negob"])
        U("ve", I("memset", onescol[:, :], 1.0), writes=["onescol"])
        U("ve", I("memset", onesm[:, :], 1.0), writes=["onesm"])
        U("ve", I("memset", onesfr[:, :], 1.0), writes=["onesfr"])
        U("ve", I("memset", onesf[:, :], 1.0), writes=["onesf"])
        U("ve", I("memset", epsap[:, :], 1e-5), writes=["epsap"])

        with nc.psum_tensor("pbias", [128, 2 * c.DC], F32) as pbias, \
             nc.psum_tensor("pbcA", [128, 512], F32) as pbcA, \
             nc.psum_tensor("pbcB", [128, 512], F32) as pbcB:
            g = []
            for j in range(c.DC):
                g.append(("matmul", (pbias[:, j:j + 1],
                                     brow[0:1, j * 128:(j + 1) * 128],
                                     onesf[0:1, 0:1]), dict(start=True, stop=True)))
            for j in range(c.DC):
                g.append(("matmul", (pbias[:, c.DC + j:c.DC + j + 1],
                                     brow[0:1, c.D + j * 128:c.D + (j + 1) * 128],
                                     onesf[0:1, 0:1]), dict(start=True, stop=True)))
            U("pe", SEQ(g, inc_idx=list(range(len(g)))),
              reads=["brq", "brk", "onesf"], writes=["pbias"],
              nincs=len(g))
            U("ve", I("tensor_copy", bqc[:, :], pbias[:, 0:c.DC]),
              reads=["pbias"], writes=["bqc"])
            U("ve", I("tensor_copy", bkc[:, :], pbias[:, c.DC:2 * c.DC]),
              reads=["pbias"], writes=["bkc"])
            for (src_off, dstbuf, rk) in ((2 * c.D, bv_bc, "brv"), (3 * c.D, bo_bc, "bro")):
                for half in range(c.D // 512):
                    pb = pbcA if half % 2 == 0 else pbcB
                    pk = f"pbc{half % 2}"
                    U("pe", I("matmul", pb[:, :], onesfr[0:1, :],
                              brow[0:1, src_off + half * 512:src_off + (half + 1) * 512],
                              start=True, stop=True),
                      reads=["onesfr", rk], writes=[pk])
                    U("ve", I("tensor_copy", dstbuf[:, half * 512:(half + 1) * 512],
                              pb[:, :]), reads=[pk], writes=[rk + "bc"])
            P.barrier()
        ph0_es.close()

        for b in range(c.B):
            with ExitStack() as ab:
                q_fm = sb(f"q_fm{b}", [128, c.DC * c.S], BF16, ab)
                k_fm = sb(f"k_fm{b}", [128, c.DC * c.S], BF16, ab)
                v_tm = sb(f"v_tm{b}", [128, c.NTB * c.H * 65], BF16, ab)
                xstr = sb(f"xstr{b}", [128, 4 * c.D], F32, ab)

                def ln_chain(i, xk, xs, xnsl, xnkey, x1key=None):
                    sk = f"st{i % 2}"
                    so = (i % 2) * 8
                    src = [x1key] if x1key else [xk]
                    U("ve", I("reduce_sum", stat[:, so:so + 1], xs, axis=AX.X),
                      reads=src, writes=[sk + "s0"])
                    U("ve", I("tensor_tensor", sqb[:, :], xs, xs, op=ALU.mult),
                      reads=src, writes=["sqb"])
                    U("ve", I("reduce_sum", stat[:, so + 1:so + 2], sqb[:, :], axis=AX.X),
                      reads=["sqb"], writes=[sk + "s1"])
                    U("ve", I("tensor_scalar", stat[:, so + 2:so + 3],
                              stat[:, so:so + 1], 1.0 / c.D, None, op0=ALU.mult),
                      reads=[sk + "s0"], writes=[sk + "m"])
                    U("ve", I("tensor_tensor", stat[:, so + 3:so + 4],
                              stat[:, so + 2:so + 3], stat[:, so + 2:so + 3],
                              op=ALU.mult),
                      reads=[sk + "m"], writes=[sk + "m2"])
                    U("ve", I("scalar_tensor_tensor", stat[:, so + 4:so + 5],
                              stat[:, so + 1:so + 2], 1.0 / c.D,
                              stat[:, so + 3:so + 4], op0=ALU.mult, op1=ALU.subtract),
                      reads=[sk + "s1", sk + "m2"], writes=[sk + "v"])
                    U("se", I("activation", stat[:, so + 5:so + 6],
                              stat[:, so + 4:so + 5], AF.Ln, bias=epsap[:, 0:1],
                              scale=1.0),
                      reads=[sk + "v", "epsap"], writes=[sk + "ln"])
                    U("se", I("activation", stat[:, so + 6:so + 7],
                              stat[:, so + 5:so + 6], AF.Exp, bias=0.0, scale=-0.5),
                      reads=[sk + "ln"], writes=[sk + "r"])
                    U("ve", I("scalar_tensor_tensor", stat[:, so + 7:so + 8],
                              stat[:, so + 2:so + 3], -1.0, stat[:, so + 6:so + 7],
                              op0=ALU.mult, op1=ALU.mult),
                      reads=[sk + "m", sk + "r"], writes=[sk + "mb"])
                    U("ve", I("tensor_scalar", xnsl, xs,
                              stat[:, so + 6:so + 7], stat[:, so + 7:so + 8],
                              op0=ALU.mult, op1=ALU.add),
                      reads=src + [sk + "mb", sk + "r"], writes=[xnkey])

                with ExitStack() as pb_, \
                     nc.psum_tensor(f"ptpA{b}", [128, 128], BF16) as ptpA, \
                     nc.psum_tensor(f"ptpB{b}", [128, 128], BF16) as ptpB:
                    xn_fm = sb(f"xn_fm{b}", [128, c.DC * c.S], BF16, pb_)
                    wbuf = sb(f"wbuf{b}", [128, c.DC * c.D], BF16, pb_)
                    xn_tm = sb(f"xn_tm{b}", [128, 2 * c.D], BF16, pb_)

                    for i in range(c.NTB):
                        xk = f"x{i % 4}"
                        xs = xstr[:, (i % 4) * c.D:(i % 4 + 1) * c.D]
                        U("sy", I("dma_start", xs, x_tile(b * c.NTB + i)),
                          writes=[xk], dma=True)
                        xnsl = xn_tm[:, (i % 2) * c.D:(i % 2 + 1) * c.D]
                        ln_chain(i, xk, xs, xnsl, f"xn_tm{i % 2}")
                        if (dbg_lv in (2, 9)) and b == 0 and i == 0:
                            U("sy", I("dma_start", dbg_t["dbg_stat"][:, 0:8],
                                      stat[:, 0:8]),
                              reads=["st0s0", "st0s1", "st0m", "st0m2", "st0v",
                                     "st0ln", "st0r", "st0mb"], dma=True)
                            U("sy", I("dma_start", dbg_t["dbg_xntm"][:, 0:c.D], xnsl),
                              reads=["xn_tm0"], dma=True)
                        for j in range(c.DC):
                            gidx = i * c.DC + j
                            pt = ptpA if gidx % 2 == 0 else ptpB
                            ptk = f"ptp{gidx % 2}"
                            U("pe", I("transpose", pt[:, :],
                                      xnsl[:, j * 128:(j + 1) * 128], ident[:, :]),
                              reads=[f"xn_tm{i % 2}", "ident"], writes=[ptk])
                            U("ve", I("tensor_copy",
                                      xn_fm[:, j * c.S + i * 128:j * c.S + (i + 1) * 128],
                                      pt[:, :]),
                              reads=[ptk], writes=["xn_fm"])

                    with nc.psum_tensor(f"ppA{b}", [128, 512], F32) as ppA, \
                         nc.psum_tensor(f"ppB{b}", [128, 512], F32) as ppB, \
                         nc.psum_tensor(f"pvA{b}", [128, 512], F32) as pvA, \
                         nc.psum_tensor(f"pvB{b}", [128, 512], F32) as pvB:
                        NS5 = c.S // 512

                        def wload(wd):
                            U("gq", I("dma_start",
                                      wbuf[:, :].rearrange("p (cc f) -> p cc f", cc=c.DC),
                                      wd.rearrange("(cc p) f -> p cc f", p=128)),
                              writes=["wbuf"], dma=True)

                        def proj_qk(dstbuf, dstkey, biasap, scale):
                            cnt = 0
                            for m in range(c.DC):
                                for t5 in range(NS5):
                                    pp = ppA if cnt % 2 == 0 else ppB
                                    ppk = f"pp{cnt % 2}"
                                    g = []
                                    for j in range(c.DC):
                                        g.append(("matmul", (
                                            pp[:, :],
                                            wbuf[:, j * c.D + m * 128:j * c.D + (m + 1) * 128],
                                            xn_fm[:, j * c.S + t5 * 512:j * c.S + (t5 + 1) * 512]),
                                            dict(start=(j == 0), stop=(j == c.DC - 1))))
                                    U("pe", SEQ(g), reads=["wbuf", "xn_fm"], writes=[ppk])
                                    U("ve", I("tensor_scalar",
                                              dstbuf[:, m * c.S + t5 * 512:m * c.S + (t5 + 1) * 512],
                                              pp[:, :],
                                              biasap[:, m:m + 1], scale,
                                              op0=ALU.add, op1=ALU.mult),
                                      reads=[ppk, "bqc", "bkc"], writes=[dstkey])
                                    cnt += 1

                        wload(wq_d)
                        proj_qk(q_fm, "q_fm", bqc, 0.125)
                        wload(wk_d)
                        proj_qk(k_fm, "k_fm", bkc, 1.0)
                        wload(wv_d)
                        U("ve", I("memset",
                                  v_tm[:, :].rearrange("p (i h ee) -> p i h ee",
                                                       i=c.NTB, h=c.H)[:, :, :, 64:65],
                                  1.0), writes=["v_ones"])
                        hh = 512 // 64
                        for i in range(c.NTB):
                            for half in range(c.D // 512):
                                pv = pvA if half % 2 == 0 else pvB
                                pvk = f"pv{half % 2}"
                                g = []
                                for j in range(c.DC):
                                    g.append(("matmul", (
                                        pv[:, :],
                                        xn_fm[:, j * c.S + i * 128:j * c.S + (i + 1) * 128],
                                        wbuf[:, j * c.D + half * 512:j * c.D + (half + 1) * 512]),
                                        dict(start=(j == 0), stop=(j == c.DC - 1))))
                                U("pe", SEQ(g), reads=["wbuf", "xn_fm"], writes=[pvk])
                                vv = v_tm[:, i * c.H * 65 + half * hh * 65:
                                          i * c.H * 65 + (half + 1) * hh * 65]
                                vv = vv.rearrange("p (h ee) -> p h ee", h=hh)[:, :, 0:64]
                                U("ve", I("tensor_tensor", vv,
                                          pv[:, :].rearrange("p (h ee) -> p h ee", h=hh),
                                          bv_bc[:, half * 512:(half + 1) * 512].rearrange(
                                              "p (h ee) -> p h ee", h=hh),
                                          op=ALU.add),
                                  reads=[pvk, "brvbc"], writes=[f"v{i}"])
                        if (dbg_lv in (3, 9)) and b == 0 and i == c.NTB - 1 and half == c.D // 512 - 1:
                            U("sy", I("dma_start", dbg_t["dbg_xn"], xn_fm[:, :]),
                              reads=["xn_fm"], dma=True)
                        if dbg_lv == 7 and b == 0 and i == c.NTB - 1 and half == c.D // 512 - 1:
                            U("gq", I("dma_start", dbg_t["dbg_xn"], xn_fm[:, :]),
                              reads=["xn_fm"], dma=True)
                            U("gq", I("dma_start", dbg_t["dbg_xntm"][:, 0:2 * c.D],
                                      xn_tm[:, :]),
                              reads=["xn_tm0", "xn_tm1"], dma=True)
                            U("gq", I("dma_start", dbg_t["dbg_stat"], stat[:, :]),
                              reads=["st0r", "st1r", "st0mb", "st1mb",
                                     "st0s0", "st0s1", "st0m", "st0m2", "st0v", "st0ln",
                                     "st1s0", "st1s1", "st1m", "st1m2", "st1v", "st1ln"],
                              dma=True)
                    P.barrier()

                if (dbg_lv in (4, 9)) and b == 0:
                    U("sy", I("dma_start", dbg_t["dbg_q"], q_fm[:, :]),
                      reads=["q_fm"], dma=True)
                    U("sy", I("dma_start", dbg_t["dbg_k"], k_fm[:, :]),
                      reads=["k_fm"], dma=True)
                    U("sy", I("dma_start", dbg_t["dbg_v"], v_tm[:, :]),
                      reads=[f"v{i}" for i in range(c.NTB)] + ["v_ones"], dma=True)

                ao_fm = sb(f"ao_fm{b}", [128, c.DC * c.S], BF16, ab)
                hd_ = ExitStack()
                pbuf = sb(f"pbuf{b}", [128, 4 * 512], BF16, hd_)
                rzsb = sb(f"rzsb{b}", [64, 512], F32, hd_)
                rzbf = sb(f"rzbf{b}", [1, 512], BF16, hd_)
                mhat = sb(f"mhat{b}", [1, c.S], BF16, hd_)

                nsq = sb(f"nsq{b}", [64, c.S], BF16, hd_)
                kb2 = sb(f"kb2{b}", [1, 2 * c.H], F32, hd_)

                with nc.psum_tensor(f"pkn{b}", [128, c.S], F32) as pkn:
                    for h in range(c.H):
                        ch, off = h // 2, (h % 2) * 64
                        ks = k_fm[off:off + 64, ch * c.S:(ch + 1) * c.S]
                        U("ve", I("tensor_tensor", nsq[0:64, :], ks, ks, op=ALU.mult),
                          reads=["k_fm"], writes=["nsq"])
                        g = []
                        for w5 in range(c.S // 512):
                            g.append(("matmul", (pkn[:, w5 * 512:(w5 + 1) * 512],
                                                 onesm[0:64, :],
                                                 nsq[0:64, w5 * 512:(w5 + 1) * 512]),
                                      dict(start=True, stop=True)))
                        U("pe", SEQ(g, inc_idx=list(range(len(g)))),
                          reads=["nsq", "onesm"], writes=["pkn"], nincs=len(g))
                        U("ve", I("reduce_max", kb2[0:1, h:h + 1], pkn[0:1, :],
                                  axis=AX.X), reads=["pkn"], writes=[f"kb2_{h}"])
                    U("se", I("activation", kb2[0:1, 0:c.H], kb2[0:1, 0:c.H], AF.Ln,
                              bias=0.0, scale=1.0),
                      reads=[f"kb2_{h}" for h in range(c.H)], writes=["kb2"])
                    U("ve", I("tensor_scalar", kb2[0:1, 0:c.H], kb2[0:1, 0:c.H],
                              0.5, 0.02, op0=ALU.mult, op1=ALU.add),
                      reads=["kb2"], writes=["kb2"])
                    P.barrier()

                qsums = sb(f"qsums{b}", [1, 2 * c.S], F32, hd_)
                mhat2 = sb(f"mhat2{b}", [1, c.S], BF16, hd_)
                nsq2 = sb(f"nsq2{b}", [64, c.S], BF16, hd_)
                with nc.psum_tensor(f"pqn{b}", [128, 512], F32) as pqn, \
                     nc.psum_tensor(f"psA{b}", [128, 512], F32) as psA, \
                     nc.psum_tensor(f"psB{b}", [128, 512], F32) as psB, \
                     nc.psum_tensor(f"psC{b}", [128, 512], F32) as psC, \
                     nc.psum_tensor(f"psD{b}", [128, 512], F32) as psD, \
                     nc.psum_tensor(f"ppv{b}", [65, 512], F32) as ppv, \
                     nc.psum_tensor(f"ppv2{b}", [65, 512], F32) as ppv2, \
                     nc.psum_tensor(f"pbcz{b}", [64, 512], F32) as pbcz:
                    PS = [psA, psB, psC, psD]
                    PPV = [ppv, ppv2]

                    pending_rz = []

                    def flush_rz():
                        while pending_rz:
                            pending_rz.pop(0)()

                    def mhat_chain(h):
                        par = h % 2
                        ch, off = h // 2, (h % 2) * 64
                        nsq_ = nsq if par == 0 else nsq2
                        mh_ = mhat if par == 0 else mhat2
                        qs = q_fm[off:off + 64, ch * c.S:(ch + 1) * c.S]
                        U("ve", I("tensor_tensor", nsq_[0:64, :], qs, qs, op=ALU.mult),
                          reads=["q_fm"], writes=[f"nsq{par}"])
                        for w5 in range(c.S // 512):
                            U("pe", I("matmul", pqn[:, :], onesm[0:64, :],
                                      nsq_[0:64, w5 * 512:(w5 + 1) * 512],
                                      start=True, stop=True),
                              reads=[f"nsq{par}", "onesm"], writes=["pqn"])
                            U("ve", I("tensor_copy",
                                      qsums[0:1, par * c.S + w5 * 512:par * c.S + (w5 + 1) * 512],
                                      pqn[0:1, :]),
                              reads=["pqn"], writes=[f"qsum{par}_{w5}", f"qln{par}"])
                        U("se", I("activation", qsums[0:1, par * c.S:(par + 1) * c.S],
                                  qsums[0:1, par * c.S:(par + 1) * c.S], AF.Ln,
                                  bias=0.0, scale=1.0),
                          reads=[f"qsum{par}_{w5}" for w5 in range(c.S // 512)],
                          writes=[f"qln{par}"])
                        U("se", I("activation", mh_[0:1, :],
                                  qsums[0:1, par * c.S:(par + 1) * c.S], AF.Exp,
                                  bias=kb2[0:1, h:h + 1], scale=0.5),
                          reads=[f"qln{par}", "kb2"], writes=[f"mhat{par}"])

                    for h in range(c.H):
                        ch, off = h // 2, (h % 2) * 64
                        if h == 0:
                            mhat_chain(0)
                        if h + 1 < c.H:
                            mhat_chain(h + 1)
                        mh_cur = mhat if h % 2 == 0 else mhat2

                        for w in range(c.NW):
                            nch = (w + 1) * (c.TW // 128)
                            diag0 = nch - (c.TW // 128)

                            ppv_c = PPV[w % 2]
                            ppk = f"ppv{w % 2}"

                            def emit_pv(a, h=h, nch=nch, ppv_c=ppv_c, ppk=ppk):
                                U("pe", I("matmul", ppv_c[:, :],
                                          v_tm[:, a * c.H * 65 + h * 65:
                                               a * c.H * 65 + (h + 1) * 65],
                                          pbuf[:, (a % 4) * 512:(a % 4 + 1) * 512],
                                          start=(a == 0), stop=(a == nch - 1)),
                                  reads=[f"P{a % 4}", f"v{a}", "v_ones"],
                                  writes=[ppk])

                            LAG = 3
                            for a in range(nch):
                                ps = PS[a % 4]
                                psk = f"ps{a % 4}"
                                g = [("matmul", (
                                    ps[:, :],
                                    k_fm[off:off + 64, ch * c.S + a * 128:ch * c.S + (a + 1) * 128],
                                    q_fm[off:off + 64, ch * c.S + w * 512:ch * c.S + (w + 1) * 512]),
                                    dict(start=True, stop=False)),
                                    ("matmul", (ps[:, :], negob[0:1, :],
                                                mh_cur[0:1, w * 512:(w + 1) * 512]),
                                     dict(start=False, stop=True))]
                                U("pe", SEQ(g),
                                  reads=["k_fm", "q_fm", f"mhat{h % 2}", "negob"],
                                  writes=[psk])
                                U("se", I("activation",
                                          pbuf[:, (a % 4) * 512:(a % 4 + 1) * 512],
                                          ps[:, :], AF.Exp, bias=0.0, scale=1.0),
                                  reads=[psk], writes=[f"P{a % 4}"])
                                if a >= diag0:
                                    delta = a * 128 - w * 512
                                    U("ve", I("tensor_tensor",
                                              pbuf[:, (a % 4) * 512:(a % 4 + 1) * 512],
                                              pbuf[:, (a % 4) * 512:(a % 4 + 1) * 512],
                                              maskb[:, 384 - delta:384 - delta + 512],
                                              op=ALU.mult),
                                      reads=[f"P{a % 4}", "maskb"], writes=[f"P{a % 4}"])
                                if a >= LAG:
                                    emit_pv(a - LAG)
                                if a == 1:
                                    flush_rz()
                            for a2 in range(max(0, nch - LAG), nch):
                                emit_pv(a2)

                            def rz_tail(h=h, w=w, ch=ch, off=off,
                                        ppv_c=ppv_c, ppk=ppk):
                                U("ve", I("reciprocal", rzsb[0:1, :],
                                          ppv_c[64:65, :]),
                                  reads=[ppk], writes=["rz"])
                                U("ve", I("tensor_copy", rzbf[0:1, :],
                                          rzsb[0:1, :]),
                                  reads=["rz"], writes=["rzbf"])
                                U("pe", I("matmul", pbcz[:, :], onesb[0:1, 0:64],
                                          rzbf[0:1, :], start=True, stop=True),
                                  reads=["rzbf", "onesb"], writes=["pbcz"])
                                U("ve", I("tensor_copy", rzsb[0:64, :],
                                          pbcz[:, :]),
                                  reads=["pbcz"], writes=["rzb64"])
                                U("ve", I("tensor_tensor",
                                          ao_fm[off:off + 64,
                                                ch * c.S + w * 512:ch * c.S + (w + 1) * 512],
                                          ppv_c[0:64, :], rzsb[0:64, :],
                                          op=ALU.mult),
                                  reads=[ppk, "rzb64"], writes=["ao_fm"])
                            pending_rz.append(rz_tail)
                    flush_rz()
                    if (dbg_lv in (6, 7)) and b == 0:
                        U("gq", I("dma_start", dbg_t["dbg_mhat"], mhat[0:1, :]),
                          reads=["mhat"], dma=True)
                        U("gq", I("dma_start", dbg_t["dbg_kb2"], kb2[0:1, 0:c.H]),
                          reads=["kb2"], dma=True)
                        U("gq", I("dma_start", dbg_t["dbg_ao"], ao_fm[:, :]),
                          reads=["ao_fm"], dma=True)
                        U("gq", I("dma_start", dbg_t["dbg_q"], q_fm[:, :]),
                          reads=["q_fm"], dma=True)
                        U("gq", I("dma_start", dbg_t["dbg_k"], k_fm[:, :]),
                          reads=["k_fm"], dma=True)
                        U("gq", I("dma_start", dbg_t["dbg_v"], v_tm[:, :]),
                          reads=[f"v{i}" for i in range(c.NTB)] + ["v_ones"], dma=True)
                    if (dbg_lv in (5, 9)) and b == 0:
                        U("sy", I("dma_start", dbg_t["dbg_mhat"], mhat[0:1, :]),
                          reads=["mhat"], dma=True)
                        U("sy", I("dma_start", dbg_t["dbg_kb2"], kb2[0:1, 0:c.H]),
                          reads=["kb2"], dma=True)
                        U("sy", I("dma_start", dbg_t["dbg_ao"], ao_fm[:, :]),
                          reads=["ao_fm"], dma=True)
                    P.barrier()

                hd_.close()

                with ExitStack() as eb, \
                     nc.psum_tensor(f"pdA0{b}", [128, 512], F32) as pdA0, \
                     nc.psum_tensor(f"pdA1{b}", [128, 512], F32) as pdA1, \
                     nc.psum_tensor(f"pdB0{b}", [128, 512], F32) as pdB0, \
                     nc.psum_tensor(f"pdB1{b}", [128, 512], F32) as pdB1, \
                     nc.psum_tensor(f"ptpA2{b}", [128, 128], BF16) as ptpA2, \
                     nc.psum_tensor(f"ptpB2{b}", [128, 128], BF16) as ptpB2, \
                     nc.psum_tensor(f"plg{b}", [128, c.E], F32) as plg:

                    wob = sb(f"wob{b}", [128, c.DC * c.D], BF16, eb)
                    x1sb = sb(f"x1sb{b}", [128, 2 * c.D], F32, eb)
                    xn1t = sb(f"xn1t{b}", [128, 2 * c.D], BF16, eb)
                    stg = sb(f"stg{b}", [128, c.DC * 512], BF16, eb)
                    U("gq", I("dma_start",
                              wob[:, :].rearrange("p (cc f) -> p cc f", cc=c.DC),
                              wo_d.rearrange("(cc p) f -> p cc f", p=128)),
                      writes=["wob"], dma=True)

                    for i in range(c.NTB):
                        gt = b * c.NTB + i
                        xk = f"ex{i % 4}"
                        xs = xstr[:, (i % 4) * c.D:(i % 4 + 1) * c.D]
                        U("sy", I("dma_start", xs, x_tile(gt)), writes=[xk], dma=True)
                        NH = c.D // 512
                        pds = [pdA0, pdA1] if i % 2 == 0 else [pdB0, pdB1]
                        pdk = [f"pd{i % 2}0", f"pd{i % 2}1"][:NH]
                        g = []
                        for j in range(c.DC):
                            lhs = ao_fm[:, j * c.S + i * 128:j * c.S + (i + 1) * 128]
                            for half in range(NH):
                                g.append(("matmul", (pds[half][:, :], lhs,
                                                     wob[:, j * c.D + half * 512:j * c.D + (half + 1) * 512]),
                                          dict(start=(j == 0), stop=(j == c.DC - 1))))
                        U("pe", SEQ(g, inc_idx=list(range(len(g) - NH, len(g)))),
                          reads=["ao_fm", "wob"], writes=pdk, nincs=NH)

                        x1sl = x1sb[:, (i % 2) * c.D:(i % 2 + 1) * c.D]
                        x1k = f"x1_{i % 2}"
                        hkeys = []
                        for half in range(NH):
                            hk = x1k + f"h{half}"
                            hkeys.append(hk)
                            U("ve", I("scalar_tensor_tensor",
                                      x1sl[:, half * 512:(half + 1) * 512],
                                      pds[half][:, :], 1.0,
                                      bo_bc[:, half * 512:(half + 1) * 512],
                                      op0=ALU.mult, op1=ALU.add),
                              reads=[pdk[half], "brobc"], writes=[hk, x1k])
                        U("ve", I("tensor_tensor", x1sl, x1sl, xs, op=ALU.add),
                          reads=hkeys + [xk], writes=[x1k])
                        xnsl = xn1t[:, (i % 2) * c.D:(i % 2 + 1) * c.D]
                        ln_chain(i, xk, x1sl, xnsl, f"xn1t{i % 2}", x1key=x1k)
                        U("sy", I("dma_start", x1_d[gt * 128:(gt + 1) * 128, :], x1sl),
                          reads=[x1k], dma=True)
                        for j in range(c.DC):
                            gidx = i * c.DC + j
                            pt = ptpA2 if gidx % 2 == 0 else ptpB2
                            ptk = f"ptp2_{gidx % 2}"
                            U("pe", I("transpose", pt[:, :],
                                      xnsl[:, j * 128:(j + 1) * 128], ident[:, :]),
                              reads=[f"xn1t{i % 2}", "ident"], writes=[ptk])
                            U("ve", I("tensor_copy",
                                      stg[:, j * 512 + (i % 4) * 128:j * 512 + (i % 4 + 1) * 128],
                                      pt[:, :]),
                              reads=[ptk], writes=[f"stg{i % 4}_{j}"])
                        g = []
                        for j in range(c.DC):
                            g.append(("matmul", (
                                plg[:, :],
                                stg[:, j * 512 + (i % 4) * 128:j * 512 + (i % 4 + 1) * 128],
                                wg8[:, j * c.E:(j + 1) * c.E]),
                                dict(start=(j == 0), stop=(j == c.DC - 1))))
                        U("pe", SEQ(g),
                          reads=[f"stg{i % 4}_{j}" for j in range(c.DC)] + ["wg8"],
                          writes=["plg"])
                        U("ve", I("tensor_copy", lgs[:, 0:c.E], plg[:, :]),
                          reads=["plg"], writes=["lg0"])
                        U("ve", I("reduce_max", gstat[:, 0:1], lgs[:, 0:c.E], axis=AX.X),
                          reads=["lg0"], writes=["gm1"])
                        U("ve", I("tensor_scalar", lgs[:, c.E:2 * c.E], lgs[:, 0:c.E],
                                  gstat[:, 0:1], None, op0=ALU.is_ge),
                          reads=["lg0", "gm1"], writes=["lg1"])
                        U("ve", I("scalar_tensor_tensor", lgs[:, c.E:2 * c.E],
                                  lgs[:, c.E:2 * c.E], -1e9, lgs[:, 0:c.E],
                                  op0=ALU.mult, op1=ALU.add),
                          reads=["lg1", "lg0"], writes=["lg1"])
                        U("ve", I("reduce_max", gstat[:, 1:2], lgs[:, c.E:2 * c.E],
                                  axis=AX.X), reads=["lg1"], writes=["gm2"])
                        U("ve", I("tensor_scalar", lgs[:, c.E:2 * c.E], lgs[:, 0:c.E],
                                  gstat[:, 1:2], None, op0=ALU.is_ge),
                          reads=["lg0", "gm2"], writes=["lg1"])
                        U("ve", I("tensor_scalar", gstat[:, 2:3], gstat[:, 0:1], -1.0,
                                  None, op0=ALU.mult), reads=["gm1"], writes=["gnm"])
                        U("se", I("activation", lgs[:, 0:c.E], lgs[:, 0:c.E], AF.Exp,
                                  bias=gstat[:, 2:3], scale=1.0),
                          reads=["lg0", "gnm"], writes=["lg0"])
                        U("ve", I("tensor_tensor", lgs[:, 0:c.E], lgs[:, 0:c.E],
                                  lgs[:, c.E:2 * c.E], op=ALU.mult),
                          reads=["lg0", "lg1"], writes=["lg0"])
                        U("ve", I("reduce_sum", gstat[:, 3:4], lgs[:, 0:c.E], axis=AX.X),
                          reads=["lg0"], writes=["gz"])
                        U("ve", I("reciprocal", gstat[:, 4:5], gstat[:, 3:4]),
                          reads=["gz"], writes=["grz"])
                        U("ve", I("tensor_scalar", gates[:, gt * c.E:(gt + 1) * c.E],
                                  lgs[:, 0:c.E], gstat[:, 4:5], None, op0=ALU.mult),
                          reads=["lg0", "grz"], writes=[f"gates{gt}"])
                        if i % 4 == 3:
                            base_t = b * c.S + (i - 3) * 128
                            U("sy", I("dma_start",
                                      xn1_d[:, :, base_t:base_t + 512].rearrange(
                                          "cc p f -> p cc f"),
                                      stg[:, :].rearrange("p (cc f) -> p cc f", cc=c.DC)),
                              reads=[f"stg{k}_{j}" for k in range(4)
                                     for j in range(c.DC)],
                              dma=True)
                    P.barrier()

        attn_es.close()

        with ExitStack() as fb, \
             nc.psum_tensor("q0", [128, 512], F32) as q0, \
             nc.psum_tensor("q1", [128, 512], F32) as q1, \
             nc.psum_tensor("q2", [128, 512], F32) as q2, \
             nc.psum_tensor("q3", [128, 512], F32) as q3, \
             nc.psum_tensor("q4", [128, 512], F32) as q4, \
             nc.psum_tensor("q5", [128, 512], F32) as q5, \
             nc.psum_tensor("q6", [128, 512], F32) as q6, \
             nc.psum_tensor("q7", [128, 512], F32) as q7:

            QQ = [q0, q1, q2, q3, q4, q5, q6, q7]
            wd_sb = sb("wd_sb", [128, c.FC * c.D], BF16, fb)
            NSLOT = 2
            wg_sb = sb("wg_sb", [128, NSLOT * c.DC * 256], BF16, fb)
            wu_sb = sb("wu_sb", [128, NSLOT * c.DC * 256], BF16, fb)
            act_sb = sb("act_sb", [128, c.FC * c.TR], BF16, fb)
            oacc = sb("oacc", [128, c.TRT * c.D], F32, fb)
            xn1q = sb("xn1q", [128, c.DC * c.TR], BF16, fb)
            sg_sb = sb("sg_sb", [128, 2 * c.TR], BF16, fb)

            U("se", I("nop"), writes=["acttbl"])

            wgu_r = [wgu_d[en].rearrange("(cc p) f -> p cc f", p=128)
                     for en in range(c.E)]
            wdn_r = [wdn_d[en].rearrange("(cc p) f -> p cc f", p=128)
                     for en in range(c.E)]
            NBLK = c.F // 256

            for p in range(c.NP):
                U("gq", I("dma_start",
                          xn1q[:, :].rearrange("p (cc tt) -> p cc tt", cc=c.DC),
                          xn1_d[:, :, p * c.TR:(p + 1) * c.TR].rearrange(
                              "cc p tt -> p cc tt")),
                  writes=["xn1q"], dma=True)
                U("sy", I("dma_start",
                          oacc[:, :].rearrange("p (i d) -> p i d", i=c.TRT),
                          x1_d[p * c.TR:(p + 1) * c.TR, :].rearrange(
                              "(i p) d -> p i d", p=128)),
                  writes=[f"oacc{t}" for t in range(c.TRT)], dma=True)

                for en in range(c.E):
                    Eg = p * c.E + en

                    def wgu_block(j, Eg=Eg, en=en):
                        slot = (Eg * NBLK + j) % NSLOT
                        for nm, buf, off in (("wg", wg_sb, 0), ("wu", wu_sb, c.F)):
                            U("gq", I("dma_start",
                                      buf[:, slot * c.DC * 256:(slot + 1) * c.DC * 256].rearrange(
                                          "p (cc f) -> p cc f", cc=c.DC),
                                      wgu_r[en][:, :, off + j * 256:off + (j + 1) * 256]),
                              writes=[f"{nm}{slot}"], dma=True)

                    def wd_quarter(qtr, en=en):
                        lo, hi = qtr * (c.FC // 4), (qtr + 1) * (c.FC // 4)
                        U("gq", I("dma_start",
                                  wd_sb[:, lo * c.D:hi * c.D].rearrange(
                                      "p (cc f) -> p cc f", cc=hi - lo),
                                  wdn_r[en][:, lo:hi, :]),
                          writes=[f"wd{qtr}"], dma=True)

                    wd_mark = [P.w[f"wd{q}"][1] if f"wd{q}" in P.w else -1
                               for q in range(4)]
                    npre = min(NSLOT, NBLK)
                    for j in range(npre):
                        wgu_block(j)

                    for fc in range(c.FC):
                        jb, sub = fc // 2, fc % 2
                        slot = (Eg * NBLK + jb) % NSLOT
                        par = fc % 2
                        gq_ = [QQ[par * 4], QQ[par * 4 + 1]]
                        uq_ = [QQ[par * 4 + 2], QQ[par * 4 + 3]]
                        gk = [f"q{par * 4}", f"q{par * 4 + 1}"]
                        uk = [f"q{par * 4 + 2}", f"q{par * 4 + 3}"]
                        base = slot * c.DC * 256 + sub * 128
                        g = []
                        for wbuf_, qq_ in ((wg_sb, gq_), (wu_sb, uq_)):
                            for j in range(c.DC):
                                lhs = wbuf_[:, base + j * 256:base + j * 256 + 128]
                                for t5 in range(c.T5):
                                    g.append(("matmul", (
                                        qq_[t5][:, :], lhs,
                                        xn1q[:, j * c.TR + t5 * 512:j * c.TR + (t5 + 1) * 512]),
                                        dict(start=(j == 0), stop=(j == c.DC - 1))))
                        half = len(g) // 2
                        U("pe", SEQ(g, inc_idx=list(range(half - c.T5, half)) +
                                    list(range(len(g) - c.T5, len(g)))),
                          reads=[f"wg{slot}", f"wu{slot}", "xn1q"],
                          writes=gk[:c.T5] + uk[:c.T5], nincs=2 * c.T5,
                          defer_drain=(fc < c.FC - 1))
                        for t5 in range(c.T5):
                            sgsl = sg_sb[:, par * c.TR + t5 * 512:par * c.TR + (t5 + 1) * 512]
                            U("se", I("activation", sgsl, gq_[t5][:, :], AF.Silu,
                                      bias=0.0, scale=1.0),
                              reads=[gk[t5], "acttbl"], writes=[f"sg{par}_{t5}"])
                            U("ve", I("tensor_tensor",
                                      act_sb[:, fc * c.TR + t5 * 512:fc * c.TR + (t5 + 1) * 512],
                                      sgsl, uq_[t5][:, :], op=ALU.mult),
                              reads=[f"sg{par}_{t5}", uk[t5]], writes=[f"act{fc}"])
                        if sub == 1 and jb + npre < NBLK:
                            wgu_block(jb + npre)
                        if sub == 1 and jb % max(1, NBLK // 4) == 1 and jb // max(1, NBLK // 4) < 4:
                            wd_quarter(jb // max(1, NBLK // 4))

                    for qtr in range((c.FC and 0) or 0, 4):
                        if f"wd{qtr}" not in P.w or P.w[f"wd{qtr}"][1] <= wd_mark[qtr]:
                            wd_quarter(qtr)
                    NSW = c.TRT // 2
                    for s in range(NSW):
                        par = s % 2
                        banks = [QQ[par * 4 + k] for k in range(4)]
                        bkeys = [f"q{par * 4 + k}" for k in range(4)]
                        NH = c.D // 512
                        g = []
                        for fc in range(c.FC):
                            for ti in range(2):
                                tile = s * 2 + ti
                                lhs = act_sb[:, fc * c.TR + tile * 128:fc * c.TR + (tile + 1) * 128]
                                for half in range(NH):
                                    g.append(("matmul", (
                                        banks[ti * 2 + half][:, :], lhs,
                                        wd_sb[:, fc * c.D + half * 512:fc * c.D + (half + 1) * 512]),
                                        dict(start=(fc == 0), stop=(fc == c.FC - 1))))
                        nch_ = 2 * NH
                        U("pe", SEQ(g, inc_idx=list(range(len(g) - nch_, len(g)))),
                          reads=[f"act{fc}" for fc in range(c.FC)] +
                                [f"wd{qtr}" for qtr in range(4)],
                          writes=bkeys, nincs=nch_,
                          defer_drain=(s < NSW - 1))
                        for ti in range(2):
                            tile = s * 2 + ti
                            gt = p * c.TRT + tile
                            for half in range(NH):
                                osl = oacc[:, tile * c.D + half * 512:tile * c.D + (half + 1) * 512]
                                U("ve", I("scalar_tensor_tensor", osl,
                                          banks[ti * 2 + half][:, :],
                                          gates[:, gt * c.E + en:gt * c.E + en + 1],
                                          osl, op0=ALU.mult, op1=ALU.add),
                                  reads=[bkeys[ti * 2 + half], f"gates{gt}",
                                         f"oacc{tile}"],
                                  writes=[f"oacc{tile}"])

                t0 = p * c.TR
                bb, rr = t0 // c.S, t0 % c.S
                U("sy", I("dma_start",
                          out_d[bb, rr:rr + c.TR, :].rearrange("(i p) d -> p i d", p=128),
                          oacc[:, :].rearrange("p (i d) -> p i d", i=c.TRT)),
                  reads=[f"oacc{t}" for t in range(c.TRT)],
                  writes=[f"outdma{p}"], dma=True)

            if dbg:
                U("sy", I("dma_start", dbg_t["dbg_gates"], gates[:, :]),
                  reads=[f"gates{gt}" for gt in range(c.NT)], dma=True)
                U("sy", I("dma_start", dbg_t["dbg_act"], act_sb[:, :]),
                  reads=[f"act{fc}" for fc in range(c.FC)], dma=True)
            U("gp", I("nop"), reads=[f"outdma{p}" for p in range(c.NP)])

        blk = es.enter_context(nc.Block())
        P.emit(blk)

    nc.finalize()
    return nc


_CACHE = {}


def kernel(**inputs):
    cfg = CFG()
    n = cfg.n_cores
    x = np.ascontiguousarray(np.asarray(inputs["L0_attn_x"], dtype=np.float32))
    bpc = x.shape[0] // n
    weights = {k: np.ascontiguousarray(np.asarray(v, dtype=np.float32))
               for k, v in inputs.items() if k != "L0_attn_x"}
    in_maps = []
    for ci in range(n):
        m = {"L0_attn_x": x[ci * bpc:(ci + 1) * bpc]}
        m.update(weights)
        in_maps.append(m)
    if "nc" not in _CACHE:
        _CACHE["nc"] = build_program(cfg)
    res = run_bass_kernel_spmd(_CACHE["nc"], in_maps, core_ids=list(range(n)))
    return np.concatenate([r["out"] for r in res.results], axis=0)


# version 11
# speedup vs baseline: 1.0326x; 1.0197x over previous
import numpy as np
import ml_dtypes
from contextlib import ExitStack

import concourse.bass as bass
import concourse.mybir as mybir
from concourse.bass_utils import run_bass_kernel_spmd

F32 = mybir.dt.float32
BF16 = mybir.dt.bfloat16
AF = mybir.ActivationFunctionType
ALU = mybir.AluOpType
AX = mybir.AxisListType


class CFG:
    def __init__(self, B=2, S=2048, D=1024, F=4096, E=8, TR=1024, n_cores=8):
        self.B, self.S, self.D, self.F, self.E, self.TR = B, S, D, F, E, TR
        self.n_cores = n_cores
        self.DK = 64
        self.H = D // 64
        self.DC = D // 128
        self.FC = F // 128
        self.TW = 512
        self.NW = S // self.TW
        self.T = B * S
        self.NT = self.T // 128
        self.NP = self.T // TR
        self.TRT = TR // 128
        self.T5 = TR // 512
        self.NTB = S // 128
        assert D % 128 == 0 and F % 512 == 0 and S % 512 == 0 and TR % 512 == 0
        assert self.T % TR == 0 and self.S % TR == 0


NCH = 8
ENG = (("pe", "ve", "se", "gp")
       + tuple(f"syc{k}" for k in range(NCH))
       + tuple(f"gqc{k}" for k in range(NCH)))
PHYS = {"pe": "tensor", "ve": "vector", "se": "scalar", "gp": "gpsimd"}
for _k in range(NCH):
    PHYS[f"syc{_k}"] = "sync"
    PHYS[f"gqc{_k}"] = "gpsimd"


def I(method, *args, **kw):
    def f(e):
        return [getattr(e, method)(*args, **kw)]
    return f


def SEQ(instrs, inc_idx=None):
    if inc_idx is None:
        inc_idx = [len(instrs) - 1]
    inc_set = set(i % len(instrs) for i in inc_idx)

    def f(e):
        out = []
        for n, (m, a, kw) in enumerate(instrs):
            ins = getattr(e, m)(*a, **kw)
            if n in inc_set:
                out.append(ins)
        return out
    return f


class Plan:
    def __init__(self, nc, es):
        self.nc = nc
        self.units = []
        self.idx = {e: 0 for e in ENG}
        self.w = {}
        self.r = {}
        self.sems = {e: es.enter_context(nc.semaphore(f"sem_{e}")) for e in ENG}
        self.floors = {e: {} for e in ENG}
        self.dmacnt = {"sy": 0, "gq": 0}
        self.pe_defer = None
        self.defer_map = {}

    def barrier(self):
        snap = dict(self.idx)
        for e in ENG:
            f = self.floors[e]
            for e2 in ENG:
                if e2 != e and snap[e2] > 0:
                    f[e2] = max(f.get(e2, 0), snap[e2])

    def unit(self, eng, fn, reads=(), writes=(), dma=False, nincs=1,
             defer_drain=False):
        if dma:
            ch = self.dmacnt[eng] % NCH
            self.dmacnt[eng] += 1
            eng = f"{eng}c{ch}"
            writes = list(writes) + [f"_{eng}"]
        need = {}

        def _need(e2, i2):
            if e2 == "pe" and eng == "pe":
                return
            if i2 > need.get(e2, 0):
                need[e2] = i2

        for k in reads:
            p = self.w.get(k)
            if p:
                _need(*p)
        for k in writes:
            p = self.w.get(k)
            if p:
                _need(*p)
            for e2, i2 in self.r.get(k, {}).items():
                _need(e2, i2)
        for e2, i2 in self.floors[eng].items():
            if i2 > need.get(e2, 0):
                need[e2] = i2
        self.floors[eng] = {}
        has_drain = eng in ("pe", "ve", "se", "gp") and not defer_drain
        if has_drain:
            nincs += 1
        inc = 16 if dma else 1
        if eng == "pe" and self.pe_defer is not None:
            self.defer_map[self.pe_defer] = self.idx[eng] + 1
            self.pe_defer = None
        self.idx[eng] += inc * nincs
        my = (eng, self.idx[eng])
        if eng == "pe" and defer_drain:
            self.pe_defer = my[1]
        for k in writes:
            self.w[k] = my
            self.r[k] = {}
        for k in reads:
            rr = self.r.setdefault(k, {})
            if rr.get(eng, 0) < my[1]:
                rr[eng] = my[1]
        self.units.append((eng, sorted(need.items()), fn, inc, nincs,
                           has_drain if eng in ("pe", "ve", "se", "gp") else False))

    def emit(self, blk):
        assert self.pe_defer is None, "last PE unit cannot defer its drain"
        streams = {v: [] for v in set(PHYS.values())}
        for u in self.units:
            streams[PHYS[u[0]]].append(u)

        def mk(items):
            def body(e):
                hw = {}
                for eng, waits, fn, inc, nincs, has_drain in items:
                    for e2, v in waits:
                        if e2 == "pe":
                            v = self.defer_map.get(v, v)
                        if hw.get(e2, 0) >= v:
                            continue
                        e.wait_ge(self.sems[e2], v)
                        hw[e2] = v
                    instrs = fn(e)
                    if has_drain:
                        instrs = instrs + [e.drain()]
                    assert len(instrs) == nincs, (eng, len(instrs), nincs)
                    for ins in instrs:
                        ins.then_inc(self.sems[eng], inc)
            return body

        for sname in ("sync", "gpsimd", "tensor", "scalar", "vector"):
            if streams.get(sname):
                getattr(blk, sname)(mk(streams[sname]))


def build_program(cfg: CFG, dbg=False) -> bass.Bass:
    c = cfg
    nc = bass.Bass("TRN2", target_bir_lowering=False)

    x_in = nc.dram_tensor("L0_attn_x", [c.B, c.S, c.D], F32, kind="ExternalInput").ap()
    wq_d = nc.dram_tensor("L0_attn_Wq", [c.D, c.D], F32, kind="ExternalInput").ap()
    bq_d = nc.dram_tensor("L0_attn_bq", [c.D], F32, kind="ExternalInput").ap()
    wk_d = nc.dram_tensor("L0_attn_Wk", [c.D, c.D], F32, kind="ExternalInput").ap()
    bk_d = nc.dram_tensor("L0_attn_bk", [c.D], F32, kind="ExternalInput").ap()
    wv_d = nc.dram_tensor("L0_attn_Wv", [c.D, c.D], F32, kind="ExternalInput").ap()
    bv_d = nc.dram_tensor("L0_attn_bv", [c.D], F32, kind="ExternalInput").ap()
    wo_d = nc.dram_tensor("L0_attn_Wo", [c.D, c.D], F32, kind="ExternalInput").ap()
    bo_d = nc.dram_tensor("L0_attn_bo", [c.D], F32, kind="ExternalInput").ap()
    wg8_d = nc.dram_tensor("L0_ffn_w_gate", [c.D, c.E], F32, kind="ExternalInput").ap()
    wgu_d = nc.dram_tensor("L0_ffn_w_gate_up", [c.E, c.D, 2 * c.F], F32,
                           kind="ExternalInput").ap()
    wdn_d = nc.dram_tensor("L0_ffn_w_down", [c.E, c.F, c.D], F32,
                           kind="ExternalInput").ap()
    out_d = nc.dram_tensor("out", [c.B, c.S, c.D], F32, kind="ExternalOutput").ap()
    scr_kind = "ExternalOutput" if dbg else "Internal"
    dbg_full = dbg is True
    dbg_lv = {False: 0, 'late': 1, 'v1': 2, 'v2': 3, 'v3': 4, 'v4': 5, 'v5': 6, 'v6': 7, True: 9}[dbg]
    x1_d = nc.dram_tensor("x1_scr", [c.T, c.D], F32, kind=scr_kind).ap()
    xn1_d = nc.dram_tensor("xn1_scr", [c.DC, 128, c.T], BF16, kind=scr_kind).ap()
    dbg_t = {}
    if dbg:
        for nm, shape, dt_ in (("dbg_xn", [128, c.DC * c.S], BF16),
                               ("dbg_q", [128, c.DC * c.S], BF16),
                               ("dbg_k", [128, c.DC * c.S], BF16),
                               ("dbg_v", [128, c.NTB * c.H * 65], BF16),
                               ("dbg_ao", [128, c.DC * c.S], BF16),
                               ("dbg_mhat", [1, c.S], BF16),
                               ("dbg_kb2", [1, c.H], F32),
                               ("dbg_gates", [128, c.NT * c.E], F32),
                               ("dbg_act", [128, c.FC * c.TR], BF16),
                               ("dbg_stat", [128, 16], F32),
                               ("dbg_xntm", [128, 2 * 1024], BF16)):
            dbg_t[nm] = nc.dram_tensor(nm, shape, dt_, kind="ExternalOutput").ap()

    ident_dram = nc.inline_tensor(np.eye(128, dtype=ml_dtypes.bfloat16), name="ident_c")

    def x_tile(t):
        b, r = t // c.NTB, (t % c.NTB) * 128
        return x_in[b, r:r + 128, :]

    es = ExitStack()
    with es:
        P = Plan(nc, es)
        U = P.unit

        def sb(name, shape, dtype, stk=es):
            return stk.enter_context(nc.sbuf_tensor(name, shape, dtype))

        ident = sb("identsb", [128, 128], BF16)
        onesb = sb("onesb", [1, 128], BF16)
        negob = sb("negob", [1, 128], BF16)
        onescol = sb("onescol", [128, 1], BF16)
        onesm = sb("onesm", [128, 128], BF16)
        onesfr = sb("onesfr", [1, 128], F32)
        onesf = sb("onesf", [1, 1], F32)
        epsap = sb("epsap", [128, 1], F32)
        maskb = sb("maskb", [128, 896], BF16)
        sqb = sb("sqb", [128, c.D], F32)
        bqc = sb("bqc", [128, c.DC], F32)
        bkc = sb("bkc", [128, c.DC], F32)
        wg8 = sb("wg8", [128, c.DC * c.E], BF16)
        gates = sb("gates", [128, c.NT * c.E], F32)
        stat = sb("stat", [128, 16], F32)
        gstat = sb("gstat", [128, 8], F32)
        lgs = sb("lgs", [128, 2 * c.E], F32)

        attn_es = ExitStack()
        bv_bc = sb("bv_bc", [128, c.D], F32, attn_es)
        bo_bc = sb("bo_bc", [128, c.D], F32, attn_es)
        ph0_es = ExitStack()
        brow = sb("brow", [1, 4 * c.D], F32, ph0_es)
        mwk = sb("mwk", [128, 896], F32, ph0_es)
        U("sy", I("dma_start", ident[:, :], ident_dram.ap()), writes=["ident"], dma=True)
        U("sy", I("dma_start", brow[0:1, 0:c.D], bq_d[None, :]), writes=["brq"], dma=True)
        U("sy", I("dma_start", brow[0:1, c.D:2 * c.D], bk_d[None, :]), writes=["brk"], dma=True)
        U("sy", I("dma_start", brow[0:1, 2 * c.D:3 * c.D], bv_d[None, :]), writes=["brv"], dma=True)
        U("sy", I("dma_start", brow[0:1, 3 * c.D:4 * c.D], bo_d[None, :]), writes=["bro"], dma=True)
        U("gq", I("dma_start", wg8[:, :].rearrange("p (cc e) -> p cc e", cc=c.DC),
                  wg8_d.rearrange("(cc p) e -> p cc e", p=128)), writes=["wg8"], dma=True)
        U("gp", I("iota", mwk[:, :], [[1, 896]], channel_multiplier=-1,
                  allow_small_or_imprecise_dtypes=True), writes=["mwk"])
        U("ve", I("tensor_scalar", mwk[:, :], mwk[:, :], 384.0, None, op0=ALU.is_ge),
          reads=["mwk"], writes=["mwk"])
        U("ve", I("tensor_copy", maskb[:, :], mwk[:, :]), reads=["mwk"], writes=["maskb"])
        U("ve", I("memset", onesb[:, :], 1.0), writes=["onesb"])
        U("ve", I("memset", negob[:, :], -1.0), writes=["negob"])
        U("ve", I("memset", onescol[:, :], 1.0), writes=["onescol"])
        U("ve", I("memset", onesm[:, :], 1.0), writes=["onesm"])
        U("ve", I("memset", onesfr[:, :], 1.0), writes=["onesfr"])
        U("ve", I("memset", onesf[:, :], 1.0), writes=["onesf"])
        U("ve", I("memset", epsap[:, :], 1e-5), writes=["epsap"])

        with nc.psum_tensor("pbias", [128, 2 * c.DC], F32) as pbias, \
             nc.psum_tensor("pbcA", [128, 512], F32) as pbcA, \
             nc.psum_tensor("pbcB", [128, 512], F32) as pbcB:
            g = []
            for j in range(c.DC):
                g.append(("matmul", (pbias[:, j:j + 1],
                                     brow[0:1, j * 128:(j + 1) * 128],
                                     onesf[0:1, 0:1]), dict(start=True, stop=True)))
            for j in range(c.DC):
                g.append(("matmul", (pbias[:, c.DC + j:c.DC + j + 1],
                                     brow[0:1, c.D + j * 128:c.D + (j + 1) * 128],
                                     onesf[0:1, 0:1]), dict(start=True, stop=True)))
            U("pe", SEQ(g, inc_idx=list(range(len(g)))),
              reads=["brq", "brk", "onesf"], writes=["pbias"],
              nincs=len(g))
            U("ve", I("tensor_copy", bqc[:, :], pbias[:, 0:c.DC]),
              reads=["pbias"], writes=["bqc"])
            U("ve", I("tensor_copy", bkc[:, :], pbias[:, c.DC:2 * c.DC]),
              reads=["pbias"], writes=["bkc"])
            for (src_off, dstbuf, rk) in ((2 * c.D, bv_bc, "brv"), (3 * c.D, bo_bc, "bro")):
                for half in range(c.D // 512):
                    pb = pbcA if half % 2 == 0 else pbcB
                    pk = f"pbc{half % 2}"
                    U("pe", I("matmul", pb[:, :], onesfr[0:1, :],
                              brow[0:1, src_off + half * 512:src_off + (half + 1) * 512],
                              start=True, stop=True),
                      reads=["onesfr", rk], writes=[pk])
                    U("ve", I("tensor_copy", dstbuf[:, half * 512:(half + 1) * 512],
                              pb[:, :]), reads=[pk], writes=[rk + "bc"])
            P.barrier()
        ph0_es.close()

        for b in range(c.B):
            with ExitStack() as ab:
                q_fm = sb(f"q_fm{b}", [128, c.DC * c.S], BF16, ab)
                k_fm = sb(f"k_fm{b}", [128, c.DC * c.S], BF16, ab)
                v_tm = sb(f"v_tm{b}", [128, c.NTB * c.H * 65], BF16, ab)
                xstr = sb(f"xstr{b}", [128, 4 * c.D], F32, ab)

                def ln_chain(i, xk, xs, xnsl, xnkey, x1key=None):
                    sk = f"st{i % 2}"
                    so = (i % 2) * 8
                    src = [x1key] if x1key else [xk]
                    U("ve", I("reduce_sum", stat[:, so:so + 1], xs, axis=AX.X),
                      reads=src, writes=[sk + "s0"])
                    U("ve", I("tensor_tensor", sqb[:, :], xs, xs, op=ALU.mult),
                      reads=src, writes=["sqb"])
                    U("ve", I("reduce_sum", stat[:, so + 1:so + 2], sqb[:, :], axis=AX.X),
                      reads=["sqb"], writes=[sk + "s1"])
                    U("ve", I("tensor_scalar", stat[:, so + 2:so + 3],
                              stat[:, so:so + 1], 1.0 / c.D, None, op0=ALU.mult),
                      reads=[sk + "s0"], writes=[sk + "m"])
                    U("ve", I("tensor_tensor", stat[:, so + 3:so + 4],
                              stat[:, so + 2:so + 3], stat[:, so + 2:so + 3],
                              op=ALU.mult),
                      reads=[sk + "m"], writes=[sk + "m2"])
                    U("ve", I("scalar_tensor_tensor", stat[:, so + 4:so + 5],
                              stat[:, so + 1:so + 2], 1.0 / c.D,
                              stat[:, so + 3:so + 4], op0=ALU.mult, op1=ALU.subtract),
                      reads=[sk + "s1", sk + "m2"], writes=[sk + "v"])
                    U("se", I("activation", stat[:, so + 5:so + 6],
                              stat[:, so + 4:so + 5], AF.Ln, bias=epsap[:, 0:1],
                              scale=1.0),
                      reads=[sk + "v", "epsap"], writes=[sk + "ln"])
                    U("se", I("activation", stat[:, so + 6:so + 7],
                              stat[:, so + 5:so + 6], AF.Exp, bias=0.0, scale=-0.5),
                      reads=[sk + "ln"], writes=[sk + "r"])
                    U("ve", I("scalar_tensor_tensor", stat[:, so + 7:so + 8],
                              stat[:, so + 2:so + 3], -1.0, stat[:, so + 6:so + 7],
                              op0=ALU.mult, op1=ALU.mult),
                      reads=[sk + "m", sk + "r"], writes=[sk + "mb"])
                    U("ve", I("tensor_scalar", xnsl, xs,
                              stat[:, so + 6:so + 7], stat[:, so + 7:so + 8],
                              op0=ALU.mult, op1=ALU.add),
                      reads=src + [sk + "mb", sk + "r"], writes=[xnkey])

                with ExitStack() as pb_, \
                     nc.psum_tensor(f"ptpA{b}", [128, 128], BF16) as ptpA, \
                     nc.psum_tensor(f"ptpB{b}", [128, 128], BF16) as ptpB:
                    xn_fm = sb(f"xn_fm{b}", [128, c.DC * c.S], BF16, pb_)
                    wbuf = sb(f"wbuf{b}", [128, c.DC * c.D], BF16, pb_)
                    xn_tm = sb(f"xn_tm{b}", [128, 2 * c.D], BF16, pb_)

                    for i in range(c.NTB):
                        xk = f"x{i % 4}"
                        xs = xstr[:, (i % 4) * c.D:(i % 4 + 1) * c.D]
                        U("sy", I("dma_start", xs, x_tile(b * c.NTB + i)),
                          writes=[xk], dma=True)
                        xnsl = xn_tm[:, (i % 2) * c.D:(i % 2 + 1) * c.D]
                        ln_chain(i, xk, xs, xnsl, f"xn_tm{i % 2}")
                        if (dbg_lv in (2, 9)) and b == 0 and i == 0:
                            U("sy", I("dma_start", dbg_t["dbg_stat"][:, 0:8],
                                      stat[:, 0:8]),
                              reads=["st0s0", "st0s1", "st0m", "st0m2", "st0v",
                                     "st0ln", "st0r", "st0mb"], dma=True)
                            U("sy", I("dma_start", dbg_t["dbg_xntm"][:, 0:c.D], xnsl),
                              reads=["xn_tm0"], dma=True)
                        for j in range(c.DC):
                            gidx = i * c.DC + j
                            pt = ptpA if gidx % 2 == 0 else ptpB
                            ptk = f"ptp{gidx % 2}"
                            U("pe", I("transpose", pt[:, :],
                                      xnsl[:, j * 128:(j + 1) * 128], ident[:, :]),
                              reads=[f"xn_tm{i % 2}", "ident"], writes=[ptk])
                            U("ve", I("tensor_copy",
                                      xn_fm[:, j * c.S + i * 128:j * c.S + (i + 1) * 128],
                                      pt[:, :]),
                              reads=[ptk], writes=["xn_fm"])

                    with nc.psum_tensor(f"ppA{b}", [128, 512], F32) as ppA, \
                         nc.psum_tensor(f"ppB{b}", [128, 512], F32) as ppB, \
                         nc.psum_tensor(f"pvA{b}", [128, 512], F32) as pvA, \
                         nc.psum_tensor(f"pvB{b}", [128, 512], F32) as pvB:
                        NS5 = c.S // 512

                        def wload(wd):
                            U("gq", I("dma_start",
                                      wbuf[:, :].rearrange("p (cc f) -> p cc f", cc=c.DC),
                                      wd.rearrange("(cc p) f -> p cc f", p=128)),
                              writes=["wbuf"], dma=True)

                        def proj_qk(dstbuf, dstkey, biasap, scale):
                            cnt = 0
                            for m in range(c.DC):
                                for t5 in range(NS5):
                                    pp = ppA if cnt % 2 == 0 else ppB
                                    ppk = f"pp{cnt % 2}"
                                    g = []
                                    for j in range(c.DC):
                                        g.append(("matmul", (
                                            pp[:, :],
                                            wbuf[:, j * c.D + m * 128:j * c.D + (m + 1) * 128],
                                            xn_fm[:, j * c.S + t5 * 512:j * c.S + (t5 + 1) * 512]),
                                            dict(start=(j == 0), stop=(j == c.DC - 1))))
                                    U("pe", SEQ(g), reads=["wbuf", "xn_fm"], writes=[ppk])
                                    U("ve", I("tensor_scalar",
                                              dstbuf[:, m * c.S + t5 * 512:m * c.S + (t5 + 1) * 512],
                                              pp[:, :],
                                              biasap[:, m:m + 1], scale,
                                              op0=ALU.add, op1=ALU.mult),
                                      reads=[ppk, "bqc", "bkc"], writes=[dstkey])
                                    cnt += 1

                        wload(wq_d)
                        proj_qk(q_fm, "q_fm", bqc, 0.125)
                        wload(wk_d)
                        proj_qk(k_fm, "k_fm", bkc, 1.0)
                        wload(wv_d)
                        U("ve", I("memset",
                                  v_tm[:, :].rearrange("p (i h ee) -> p i h ee",
                                                       i=c.NTB, h=c.H)[:, :, :, 64:65],
                                  1.0), writes=["v_ones"])
                        hh = 512 // 64
                        for i in range(c.NTB):
                            for half in range(c.D // 512):
                                pv = pvA if half % 2 == 0 else pvB
                                pvk = f"pv{half % 2}"
                                g = []
                                for j in range(c.DC):
                                    g.append(("matmul", (
                                        pv[:, :],
                                        xn_fm[:, j * c.S + i * 128:j * c.S + (i + 1) * 128],
                                        wbuf[:, j * c.D + half * 512:j * c.D + (half + 1) * 512]),
                                        dict(start=(j == 0), stop=(j == c.DC - 1))))
                                U("pe", SEQ(g), reads=["wbuf", "xn_fm"], writes=[pvk])
                                vv = v_tm[:, i * c.H * 65 + half * hh * 65:
                                          i * c.H * 65 + (half + 1) * hh * 65]
                                vv = vv.rearrange("p (h ee) -> p h ee", h=hh)[:, :, 0:64]
                                U("ve", I("tensor_tensor", vv,
                                          pv[:, :].rearrange("p (h ee) -> p h ee", h=hh),
                                          bv_bc[:, half * 512:(half + 1) * 512].rearrange(
                                              "p (h ee) -> p h ee", h=hh),
                                          op=ALU.add),
                                  reads=[pvk, "brvbc"], writes=[f"v{i}"])
                        if (dbg_lv in (3, 9)) and b == 0 and i == c.NTB - 1 and half == c.D // 512 - 1:
                            U("sy", I("dma_start", dbg_t["dbg_xn"], xn_fm[:, :]),
                              reads=["xn_fm"], dma=True)
                        if dbg_lv == 7 and b == 0 and i == c.NTB - 1 and half == c.D // 512 - 1:
                            U("gq", I("dma_start", dbg_t["dbg_xn"], xn_fm[:, :]),
                              reads=["xn_fm"], dma=True)
                            U("gq", I("dma_start", dbg_t["dbg_xntm"][:, 0:2 * c.D],
                                      xn_tm[:, :]),
                              reads=["xn_tm0", "xn_tm1"], dma=True)
                            U("gq", I("dma_start", dbg_t["dbg_stat"], stat[:, :]),
                              reads=["st0r", "st1r", "st0mb", "st1mb",
                                     "st0s0", "st0s1", "st0m", "st0m2", "st0v", "st0ln",
                                     "st1s0", "st1s1", "st1m", "st1m2", "st1v", "st1ln"],
                              dma=True)
                    P.barrier()

                if (dbg_lv in (4, 9)) and b == 0:
                    U("sy", I("dma_start", dbg_t["dbg_q"], q_fm[:, :]),
                      reads=["q_fm"], dma=True)
                    U("sy", I("dma_start", dbg_t["dbg_k"], k_fm[:, :]),
                      reads=["k_fm"], dma=True)
                    U("sy", I("dma_start", dbg_t["dbg_v"], v_tm[:, :]),
                      reads=[f"v{i}" for i in range(c.NTB)] + ["v_ones"], dma=True)

                ao_fm = sb(f"ao_fm{b}", [128, c.DC * c.S], BF16, ab)
                hd_ = ExitStack()
                pbuf = sb(f"pbuf{b}", [128, 4 * 512], BF16, hd_)
                rzsb = sb(f"rzsb{b}", [64, 512], F32, hd_)
                rzbf = sb(f"rzbf{b}", [1, 512], BF16, hd_)
                mhat = sb(f"mhat{b}", [1, c.S], BF16, hd_)

                nsq = sb(f"nsq{b}", [64, c.S], BF16, hd_)
                nsq2 = sb(f"nsq2{b}", [64, c.S], BF16, hd_)
                kb2 = sb(f"kb2{b}", [1, 2 * c.H], F32, hd_)

                with nc.psum_tensor(f"pkn{b}", [128, c.S], F32) as pkn, \
                     nc.psum_tensor(f"pkn2{b}", [128, c.S], F32) as pkn2:
                    for h in range(c.H):
                        ch, off = h // 2, (h % 2) * 64
                        nsq_ = nsq if h % 2 == 0 else nsq2
                        pk_ = pkn if h % 2 == 0 else pkn2
                        pkk = f"pkn{h % 2}"
                        ks = k_fm[off:off + 64, ch * c.S:(ch + 1) * c.S]
                        U("ve", I("tensor_tensor", nsq_[0:64, :], ks, ks, op=ALU.mult),
                          reads=["k_fm"], writes=[f"nsq{h % 2}"])
                        g = []
                        for w5 in range(c.S // 512):
                            g.append(("matmul", (pk_[:, w5 * 512:(w5 + 1) * 512],
                                                 onesm[0:64, :],
                                                 nsq_[0:64, w5 * 512:(w5 + 1) * 512]),
                                      dict(start=True, stop=True)))
                        U("pe", SEQ(g, inc_idx=list(range(len(g)))),
                          reads=[f"nsq{h % 2}", "onesm"], writes=[pkk], nincs=len(g))
                        U("ve", I("reduce_max", kb2[0:1, h:h + 1], pk_[0:1, :],
                                  axis=AX.X), reads=[pkk], writes=[f"kb2_{h}"])
                    U("se", I("activation", kb2[0:1, 0:c.H], kb2[0:1, 0:c.H], AF.Ln,
                              bias=0.0, scale=1.0),
                      reads=[f"kb2_{h}" for h in range(c.H)], writes=["kb2"])
                    U("ve", I("tensor_scalar", kb2[0:1, 0:c.H], kb2[0:1, 0:c.H],
                              0.5, 0.02, op0=ALU.mult, op1=ALU.add),
                      reads=["kb2"], writes=["kb2"])
                    P.barrier()

                qsums = sb(f"qsums{b}", [1, 2 * c.S], F32, hd_)
                mhat2 = sb(f"mhat2{b}", [1, c.S], BF16, hd_)
                with nc.psum_tensor(f"pqn{b}", [128, 512], F32) as pqn, \
                     nc.psum_tensor(f"psA{b}", [128, 512], F32) as psA, \
                     nc.psum_tensor(f"psB{b}", [128, 512], F32) as psB, \
                     nc.psum_tensor(f"psC{b}", [128, 512], F32) as psC, \
                     nc.psum_tensor(f"psD{b}", [128, 512], F32) as psD, \
                     nc.psum_tensor(f"ppv{b}", [65, 512], F32) as ppv, \
                     nc.psum_tensor(f"ppv2{b}", [65, 512], F32) as ppv2, \
                     nc.psum_tensor(f"pbcz{b}", [64, 512], F32) as pbcz:
                    PS = [psA, psB, psC, psD]
                    PPV = [ppv, ppv2]

                    pending_rz = []

                    def flush_rz():
                        while pending_rz:
                            pending_rz.pop(0)()

                    def mhat_chain(h):
                        par = h % 2
                        ch, off = h // 2, (h % 2) * 64
                        nsq_ = nsq if par == 0 else nsq2
                        mh_ = mhat if par == 0 else mhat2
                        qs = q_fm[off:off + 64, ch * c.S:(ch + 1) * c.S]
                        U("ve", I("tensor_tensor", nsq_[0:64, :], qs, qs, op=ALU.mult),
                          reads=["q_fm"], writes=[f"nsq{par}"])
                        for w5 in range(c.S // 512):
                            U("pe", I("matmul", pqn[:, :], onesm[0:64, :],
                                      nsq_[0:64, w5 * 512:(w5 + 1) * 512],
                                      start=True, stop=True),
                              reads=[f"nsq{par}", "onesm"], writes=["pqn"])
                            U("ve", I("tensor_copy",
                                      qsums[0:1, par * c.S + w5 * 512:par * c.S + (w5 + 1) * 512],
                                      pqn[0:1, :]),
                              reads=["pqn"], writes=[f"qsum{par}_{w5}", f"qln{par}"])
                        U("se", I("activation", qsums[0:1, par * c.S:(par + 1) * c.S],
                                  qsums[0:1, par * c.S:(par + 1) * c.S], AF.Ln,
                                  bias=0.0, scale=1.0),
                          reads=[f"qsum{par}_{w5}" for w5 in range(c.S // 512)],
                          writes=[f"qln{par}"])
                        U("se", I("activation", mh_[0:1, :],
                                  qsums[0:1, par * c.S:(par + 1) * c.S], AF.Exp,
                                  bias=kb2[0:1, h:h + 1], scale=0.5),
                          reads=[f"qln{par}", "kb2"], writes=[f"mhat{par}"])

                    for h in range(c.H):
                        ch, off = h // 2, (h % 2) * 64
                        if h == 0:
                            mhat_chain(0)
                        if h + 1 < c.H:
                            mhat_chain(h + 1)
                        mh_cur = mhat if h % 2 == 0 else mhat2

                        for w in range(c.NW):
                            nch = (w + 1) * (c.TW // 128)
                            diag0 = nch - (c.TW // 128)

                            ppv_c = PPV[w % 2]
                            ppk = f"ppv{w % 2}"

                            def emit_pv(a, h=h, nch=nch, ppv_c=ppv_c, ppk=ppk):
                                U("pe", I("matmul", ppv_c[:, :],
                                          v_tm[:, a * c.H * 65 + h * 65:
                                               a * c.H * 65 + (h + 1) * 65],
                                          pbuf[:, (a % 4) * 512:(a % 4 + 1) * 512],
                                          start=(a == 0), stop=(a == nch - 1)),
                                  reads=[f"P{a % 4}", f"v{a}", "v_ones"],
                                  writes=[ppk])

                            LAG = 3
                            for a in range(nch):
                                ps = PS[a % 4]
                                psk = f"ps{a % 4}"
                                g = [("matmul", (
                                    ps[:, :],
                                    k_fm[off:off + 64, ch * c.S + a * 128:ch * c.S + (a + 1) * 128],
                                    q_fm[off:off + 64, ch * c.S + w * 512:ch * c.S + (w + 1) * 512]),
                                    dict(start=True, stop=False)),
                                    ("matmul", (ps[:, :], negob[0:1, :],
                                                mh_cur[0:1, w * 512:(w + 1) * 512]),
                                     dict(start=False, stop=True))]
                                U("pe", SEQ(g),
                                  reads=["k_fm", "q_fm", f"mhat{h % 2}", "negob"],
                                  writes=[psk])
                                U("se", I("activation",
                                          pbuf[:, (a % 4) * 512:(a % 4 + 1) * 512],
                                          ps[:, :], AF.Exp, bias=0.0, scale=1.0),
                                  reads=[psk], writes=[f"P{a % 4}"])
                                if a >= diag0:
                                    delta = a * 128 - w * 512
                                    U("ve", I("tensor_tensor",
                                              pbuf[:, (a % 4) * 512:(a % 4 + 1) * 512],
                                              pbuf[:, (a % 4) * 512:(a % 4 + 1) * 512],
                                              maskb[:, 384 - delta:384 - delta + 512],
                                              op=ALU.mult),
                                      reads=[f"P{a % 4}", "maskb"], writes=[f"P{a % 4}"])
                                if a >= LAG:
                                    emit_pv(a - LAG)
                                if a == 1:
                                    flush_rz()
                            for a2 in range(max(0, nch - LAG), nch):
                                emit_pv(a2)

                            def rz_tail(h=h, w=w, ch=ch, off=off,
                                        ppv_c=ppv_c, ppk=ppk):
                                U("ve", I("reciprocal", rzsb[0:1, :],
                                          ppv_c[64:65, :]),
                                  reads=[ppk], writes=["rz"])
                                U("ve", I("tensor_copy", rzbf[0:1, :],
                                          rzsb[0:1, :]),
                                  reads=["rz"], writes=["rzbf"])
                                U("pe", I("matmul", pbcz[:, :], onesb[0:1, 0:64],
                                          rzbf[0:1, :], start=True, stop=True),
                                  reads=["rzbf", "onesb"], writes=["pbcz"])
                                U("ve", I("tensor_copy", rzsb[0:64, :],
                                          pbcz[:, :]),
                                  reads=["pbcz"], writes=["rzb64"])
                                U("ve", I("tensor_tensor",
                                          ao_fm[off:off + 64,
                                                ch * c.S + w * 512:ch * c.S + (w + 1) * 512],
                                          ppv_c[0:64, :], rzsb[0:64, :],
                                          op=ALU.mult),
                                  reads=[ppk, "rzb64"], writes=["ao_fm"])
                            pending_rz.append(rz_tail)
                    flush_rz()
                    if (dbg_lv in (6, 7)) and b == 0:
                        U("gq", I("dma_start", dbg_t["dbg_mhat"], mhat[0:1, :]),
                          reads=["mhat"], dma=True)
                        U("gq", I("dma_start", dbg_t["dbg_kb2"], kb2[0:1, 0:c.H]),
                          reads=["kb2"], dma=True)
                        U("gq", I("dma_start", dbg_t["dbg_ao"], ao_fm[:, :]),
                          reads=["ao_fm"], dma=True)
                        U("gq", I("dma_start", dbg_t["dbg_q"], q_fm[:, :]),
                          reads=["q_fm"], dma=True)
                        U("gq", I("dma_start", dbg_t["dbg_k"], k_fm[:, :]),
                          reads=["k_fm"], dma=True)
                        U("gq", I("dma_start", dbg_t["dbg_v"], v_tm[:, :]),
                          reads=[f"v{i}" for i in range(c.NTB)] + ["v_ones"], dma=True)
                    if (dbg_lv in (5, 9)) and b == 0:
                        U("sy", I("dma_start", dbg_t["dbg_mhat"], mhat[0:1, :]),
                          reads=["mhat"], dma=True)
                        U("sy", I("dma_start", dbg_t["dbg_kb2"], kb2[0:1, 0:c.H]),
                          reads=["kb2"], dma=True)
                        U("sy", I("dma_start", dbg_t["dbg_ao"], ao_fm[:, :]),
                          reads=["ao_fm"], dma=True)
                    P.barrier()

                hd_.close()

                with ExitStack() as eb, \
                     nc.psum_tensor(f"pdA0{b}", [128, 512], F32) as pdA0, \
                     nc.psum_tensor(f"pdA1{b}", [128, 512], F32) as pdA1, \
                     nc.psum_tensor(f"pdB0{b}", [128, 512], F32) as pdB0, \
                     nc.psum_tensor(f"pdB1{b}", [128, 512], F32) as pdB1, \
                     nc.psum_tensor(f"ptpA2{b}", [128, 128], BF16) as ptpA2, \
                     nc.psum_tensor(f"ptpB2{b}", [128, 128], BF16) as ptpB2, \
                     nc.psum_tensor(f"plg{b}", [128, c.E], F32) as plg:

                    wob = sb(f"wob{b}", [128, c.DC * c.D], BF16, eb)
                    x1sb = sb(f"x1sb{b}", [128, 2 * c.D], F32, eb)
                    xn1t = sb(f"xn1t{b}", [128, 2 * c.D], BF16, eb)
                    stg = sb(f"stg{b}", [128, c.DC * 512], BF16, eb)
                    U("gq", I("dma_start",
                              wob[:, :].rearrange("p (cc f) -> p cc f", cc=c.DC),
                              wo_d.rearrange("(cc p) f -> p cc f", p=128)),
                      writes=["wob"], dma=True)

                    for i in range(c.NTB):
                        gt = b * c.NTB + i
                        xk = f"ex{i % 4}"
                        xs = xstr[:, (i % 4) * c.D:(i % 4 + 1) * c.D]
                        U("sy", I("dma_start", xs, x_tile(gt)), writes=[xk], dma=True)
                        NH = c.D // 512
                        pds = [pdA0, pdA1] if i % 2 == 0 else [pdB0, pdB1]
                        pdk = [f"pd{i % 2}0", f"pd{i % 2}1"][:NH]
                        g = []
                        for j in range(c.DC):
                            lhs = ao_fm[:, j * c.S + i * 128:j * c.S + (i + 1) * 128]
                            for half in range(NH):
                                g.append(("matmul", (pds[half][:, :], lhs,
                                                     wob[:, j * c.D + half * 512:j * c.D + (half + 1) * 512]),
                                          dict(start=(j == 0), stop=(j == c.DC - 1))))
                        U("pe", SEQ(g, inc_idx=list(range(len(g) - NH, len(g)))),
                          reads=["ao_fm", "wob"], writes=pdk, nincs=NH)

                        x1sl = x1sb[:, (i % 2) * c.D:(i % 2 + 1) * c.D]
                        x1k = f"x1_{i % 2}"
                        hkeys = []
                        for half in range(NH):
                            hk = x1k + f"h{half}"
                            hkeys.append(hk)
                            U("ve", I("scalar_tensor_tensor",
                                      x1sl[:, half * 512:(half + 1) * 512],
                                      pds[half][:, :], 1.0,
                                      bo_bc[:, half * 512:(half + 1) * 512],
                                      op0=ALU.mult, op1=ALU.add),
                              reads=[pdk[half], "brobc"], writes=[hk, x1k])
                        U("ve", I("tensor_tensor", x1sl, x1sl, xs, op=ALU.add),
                          reads=hkeys + [xk], writes=[x1k])
                        xnsl = xn1t[:, (i % 2) * c.D:(i % 2 + 1) * c.D]
                        ln_chain(i, xk, x1sl, xnsl, f"xn1t{i % 2}", x1key=x1k)
                        U("sy", I("dma_start", x1_d[gt * 128:(gt + 1) * 128, :], x1sl),
                          reads=[x1k], dma=True)
                        for j in range(c.DC):
                            gidx = i * c.DC + j
                            pt = ptpA2 if gidx % 2 == 0 else ptpB2
                            ptk = f"ptp2_{gidx % 2}"
                            U("pe", I("transpose", pt[:, :],
                                      xnsl[:, j * 128:(j + 1) * 128], ident[:, :]),
                              reads=[f"xn1t{i % 2}", "ident"], writes=[ptk])
                            U("ve", I("tensor_copy",
                                      stg[:, j * 512 + (i % 4) * 128:j * 512 + (i % 4 + 1) * 128],
                                      pt[:, :]),
                              reads=[ptk], writes=[f"stg{i % 4}_{j}"])
                        g = []
                        for j in range(c.DC):
                            g.append(("matmul", (
                                plg[:, :],
                                stg[:, j * 512 + (i % 4) * 128:j * 512 + (i % 4 + 1) * 128],
                                wg8[:, j * c.E:(j + 1) * c.E]),
                                dict(start=(j == 0), stop=(j == c.DC - 1))))
                        U("pe", SEQ(g),
                          reads=[f"stg{i % 4}_{j}" for j in range(c.DC)] + ["wg8"],
                          writes=["plg"])
                        U("ve", I("tensor_copy", lgs[:, 0:c.E], plg[:, :]),
                          reads=["plg"], writes=["lg0"])
                        U("ve", I("reduce_max", gstat[:, 0:1], lgs[:, 0:c.E], axis=AX.X),
                          reads=["lg0"], writes=["gm1"])
                        U("ve", I("tensor_scalar", lgs[:, c.E:2 * c.E], lgs[:, 0:c.E],
                                  gstat[:, 0:1], None, op0=ALU.is_ge),
                          reads=["lg0", "gm1"], writes=["lg1"])
                        U("ve", I("scalar_tensor_tensor", lgs[:, c.E:2 * c.E],
                                  lgs[:, c.E:2 * c.E], -1e9, lgs[:, 0:c.E],
                                  op0=ALU.mult, op1=ALU.add),
                          reads=["lg1", "lg0"], writes=["lg1"])
                        U("ve", I("reduce_max", gstat[:, 1:2], lgs[:, c.E:2 * c.E],
                                  axis=AX.X), reads=["lg1"], writes=["gm2"])
                        U("ve", I("tensor_scalar", lgs[:, c.E:2 * c.E], lgs[:, 0:c.E],
                                  gstat[:, 1:2], None, op0=ALU.is_ge),
                          reads=["lg0", "gm2"], writes=["lg1"])
                        U("ve", I("tensor_scalar", gstat[:, 2:3], gstat[:, 0:1], -1.0,
                                  None, op0=ALU.mult), reads=["gm1"], writes=["gnm"])
                        U("se", I("activation", lgs[:, 0:c.E], lgs[:, 0:c.E], AF.Exp,
                                  bias=gstat[:, 2:3], scale=1.0),
                          reads=["lg0", "gnm"], writes=["lg0"])
                        U("ve", I("tensor_tensor", lgs[:, 0:c.E], lgs[:, 0:c.E],
                                  lgs[:, c.E:2 * c.E], op=ALU.mult),
                          reads=["lg0", "lg1"], writes=["lg0"])
                        U("ve", I("reduce_sum", gstat[:, 3:4], lgs[:, 0:c.E], axis=AX.X),
                          reads=["lg0"], writes=["gz"])
                        U("ve", I("reciprocal", gstat[:, 4:5], gstat[:, 3:4]),
                          reads=["gz"], writes=["grz"])
                        U("ve", I("tensor_scalar", gates[:, gt * c.E:(gt + 1) * c.E],
                                  lgs[:, 0:c.E], gstat[:, 4:5], None, op0=ALU.mult),
                          reads=["lg0", "grz"], writes=[f"gates{gt}"])
                        if i % 4 == 3:
                            base_t = b * c.S + (i - 3) * 128
                            U("sy", I("dma_start",
                                      xn1_d[:, :, base_t:base_t + 512].rearrange(
                                          "cc p f -> p cc f"),
                                      stg[:, :].rearrange("p (cc f) -> p cc f", cc=c.DC)),
                              reads=[f"stg{k}_{j}" for k in range(4)
                                     for j in range(c.DC)],
                              dma=True)
                    P.barrier()

        attn_es.close()

        with ExitStack() as fb, \
             nc.psum_tensor("q0", [128, 512], F32) as q0, \
             nc.psum_tensor("q1", [128, 512], F32) as q1, \
             nc.psum_tensor("q2", [128, 512], F32) as q2, \
             nc.psum_tensor("q3", [128, 512], F32) as q3, \
             nc.psum_tensor("q4", [128, 512], F32) as q4, \
             nc.psum_tensor("q5", [128, 512], F32) as q5, \
             nc.psum_tensor("q6", [128, 512], F32) as q6, \
             nc.psum_tensor("q7", [128, 512], F32) as q7:

            QQ = [q0, q1, q2, q3, q4, q5, q6, q7]
            wd_sb = sb("wd_sb", [128, c.FC * c.D], BF16, fb)
            NSLOT = 2
            wg_sb = sb("wg_sb", [128, NSLOT * c.DC * 256], BF16, fb)
            wu_sb = sb("wu_sb", [128, NSLOT * c.DC * 256], BF16, fb)
            act_sb = sb("act_sb", [128, c.FC * c.TR], BF16, fb)
            oacc = sb("oacc", [128, c.TRT * c.D], F32, fb)
            xn1q = sb("xn1q", [128, c.DC * c.TR], BF16, fb)
            sg_sb = sb("sg_sb", [128, 2 * c.TR], BF16, fb)

            U("se", I("nop"), writes=["acttbl"])

            wgu_r = [wgu_d[en].rearrange("(cc p) f -> p cc f", p=128)
                     for en in range(c.E)]
            wdn_r = [wdn_d[en].rearrange("(cc p) f -> p cc f", p=128)
                     for en in range(c.E)]
            NBLK = c.F // 256

            for p in range(c.NP):
                U("gq", I("dma_start",
                          xn1q[:, :].rearrange("p (cc tt) -> p cc tt", cc=c.DC),
                          xn1_d[:, :, p * c.TR:(p + 1) * c.TR].rearrange(
                              "cc p tt -> p cc tt")),
                  writes=["xn1q"], dma=True)
                U("sy", I("dma_start",
                          oacc[:, :].rearrange("p (i d) -> p i d", i=c.TRT),
                          x1_d[p * c.TR:(p + 1) * c.TR, :].rearrange(
                              "(i p) d -> p i d", p=128)),
                  writes=[f"oacc{t}" for t in range(c.TRT)], dma=True)

                for en in range(c.E):
                    Eg = p * c.E + en

                    def wgu_block(j, Eg=Eg, en=en):
                        slot = (Eg * NBLK + j) % NSLOT
                        for nm, buf, off in (("wg", wg_sb, 0), ("wu", wu_sb, c.F)):
                            U("gq", I("dma_start",
                                      buf[:, slot * c.DC * 256:(slot + 1) * c.DC * 256].rearrange(
                                          "p (cc f) -> p cc f", cc=c.DC),
                                      wgu_r[en][:, :, off + j * 256:off + (j + 1) * 256]),
                              writes=[f"{nm}{slot}"], dma=True)

                    def wd_quarter(qtr, en=en):
                        lo, hi = qtr * (c.FC // 4), (qtr + 1) * (c.FC // 4)
                        U("gq", I("dma_start",
                                  wd_sb[:, lo * c.D:hi * c.D].rearrange(
                                      "p (cc f) -> p cc f", cc=hi - lo),
                                  wdn_r[en][:, lo:hi, :]),
                          writes=[f"wd{qtr}"], dma=True)

                    wd_mark = [P.w[f"wd{q}"][1] if f"wd{q}" in P.w else -1
                               for q in range(4)]
                    npre = min(NSLOT, NBLK)
                    for j in range(npre):
                        wgu_block(j)

                    for fc in range(c.FC):
                        jb, sub = fc // 2, fc % 2
                        slot = (Eg * NBLK + jb) % NSLOT
                        par = fc % 2
                        gq_ = [QQ[par * 4], QQ[par * 4 + 1]]
                        uq_ = [QQ[par * 4 + 2], QQ[par * 4 + 3]]
                        gk = [f"q{par * 4}", f"q{par * 4 + 1}"]
                        uk = [f"q{par * 4 + 2}", f"q{par * 4 + 3}"]
                        base = slot * c.DC * 256 + sub * 128
                        g = []
                        for wbuf_, qq_ in ((wg_sb, gq_), (wu_sb, uq_)):
                            for j in range(c.DC):
                                lhs = wbuf_[:, base + j * 256:base + j * 256 + 128]
                                for t5 in range(c.T5):
                                    g.append(("matmul", (
                                        qq_[t5][:, :], lhs,
                                        xn1q[:, j * c.TR + t5 * 512:j * c.TR + (t5 + 1) * 512]),
                                        dict(start=(j == 0), stop=(j == c.DC - 1))))
                        half = len(g) // 2
                        U("pe", SEQ(g, inc_idx=list(range(half - c.T5, half)) +
                                    list(range(len(g) - c.T5, len(g)))),
                          reads=[f"wg{slot}", f"wu{slot}", "xn1q"],
                          writes=gk[:c.T5] + uk[:c.T5], nincs=2 * c.T5,
                          defer_drain=(fc < c.FC - 1))
                        for t5 in range(c.T5):
                            sgsl = sg_sb[:, par * c.TR + t5 * 512:par * c.TR + (t5 + 1) * 512]
                            U("se", I("activation", sgsl, gq_[t5][:, :], AF.Silu,
                                      bias=0.0, scale=1.0),
                              reads=[gk[t5], "acttbl"], writes=[f"sg{par}_{t5}"])
                            U("ve", I("tensor_tensor",
                                      act_sb[:, fc * c.TR + t5 * 512:fc * c.TR + (t5 + 1) * 512],
                                      sgsl, uq_[t5][:, :], op=ALU.mult),
                              reads=[f"sg{par}_{t5}", uk[t5]], writes=[f"act{fc}"])
                        if sub == 1 and jb + npre < NBLK:
                            wgu_block(jb + npre)
                        if sub == 1 and jb % max(1, NBLK // 4) == 1 and jb // max(1, NBLK // 4) < 4:
                            wd_quarter(jb // max(1, NBLK // 4))

                    for qtr in range((c.FC and 0) or 0, 4):
                        if f"wd{qtr}" not in P.w or P.w[f"wd{qtr}"][1] <= wd_mark[qtr]:
                            wd_quarter(qtr)
                    NSW = c.TRT // 2
                    for s in range(NSW):
                        par = s % 2
                        banks = [QQ[par * 4 + k] for k in range(4)]
                        bkeys = [f"q{par * 4 + k}" for k in range(4)]
                        NH = c.D // 512
                        g = []
                        for fc in range(c.FC):
                            for ti in range(2):
                                tile = s * 2 + ti
                                lhs = act_sb[:, fc * c.TR + tile * 128:fc * c.TR + (tile + 1) * 128]
                                for half in range(NH):
                                    g.append(("matmul", (
                                        banks[ti * 2 + half][:, :], lhs,
                                        wd_sb[:, fc * c.D + half * 512:fc * c.D + (half + 1) * 512]),
                                        dict(start=(fc == 0), stop=(fc == c.FC - 1))))
                        nch_ = 2 * NH
                        U("pe", SEQ(g, inc_idx=list(range(len(g) - nch_, len(g)))),
                          reads=[f"act{fc}" for fc in range(c.FC)] +
                                [f"wd{qtr}" for qtr in range(4)],
                          writes=bkeys, nincs=nch_,
                          defer_drain=(s < NSW - 1))
                        for ti in range(2):
                            tile = s * 2 + ti
                            gt = p * c.TRT + tile
                            for half in range(NH):
                                osl = oacc[:, tile * c.D + half * 512:tile * c.D + (half + 1) * 512]
                                U("ve", I("scalar_tensor_tensor", osl,
                                          banks[ti * 2 + half][:, :],
                                          gates[:, gt * c.E + en:gt * c.E + en + 1],
                                          osl, op0=ALU.mult, op1=ALU.add),
                                  reads=[bkeys[ti * 2 + half], f"gates{gt}",
                                         f"oacc{tile}"],
                                  writes=[f"oacc{tile}"])

                t0 = p * c.TR
                bb, rr = t0 // c.S, t0 % c.S
                U("sy", I("dma_start",
                          out_d[bb, rr:rr + c.TR, :].rearrange("(i p) d -> p i d", p=128),
                          oacc[:, :].rearrange("p (i d) -> p i d", i=c.TRT)),
                  reads=[f"oacc{t}" for t in range(c.TRT)],
                  writes=[f"outdma{p}"], dma=True)

            if dbg:
                U("sy", I("dma_start", dbg_t["dbg_gates"], gates[:, :]),
                  reads=[f"gates{gt}" for gt in range(c.NT)], dma=True)
                U("sy", I("dma_start", dbg_t["dbg_act"], act_sb[:, :]),
                  reads=[f"act{fc}" for fc in range(c.FC)], dma=True)
            U("gp", I("nop"), reads=[f"outdma{p}" for p in range(c.NP)])

        blk = es.enter_context(nc.Block())
        P.emit(blk)

    nc.finalize()
    return nc


_CACHE = {}


def kernel(**inputs):
    cfg = CFG()
    n = cfg.n_cores
    x = np.ascontiguousarray(np.asarray(inputs["L0_attn_x"], dtype=np.float32))
    bpc = x.shape[0] // n
    weights = {k: np.ascontiguousarray(np.asarray(v, dtype=np.float32))
               for k, v in inputs.items() if k != "L0_attn_x"}
    in_maps = []
    for ci in range(n):
        m = {"L0_attn_x": x[ci * bpc:(ci + 1) * bpc]}
        m.update(weights)
        in_maps.append(m)
    if "nc" not in _CACHE:
        _CACHE["nc"] = build_program(cfg)
    res = run_bass_kernel_spmd(_CACHE["nc"], in_maps, core_ids=list(range(n)))
    return np.concatenate([r["out"] for r in res.results], axis=0)
